# version 12
# baseline (speedup 1.0000x reference)
import contextlib
import numpy as np
import ml_dtypes
import concourse.bass as bass
import concourse.mybir as mybir
from concourse.bass_utils import run_bass_kernel_spmd

F32 = mybir.dt.float32
BF16 = mybir.dt.bfloat16
I8 = mybir.dt.int8
AF = mybir.ActivationFunctionType
ALU = mybir.AluOpType
ET = mybir.EngineType

H = 2048
NH = 16
INTER = 8192
NC = 8
KC = 16
MCG = 64
SCALE = 128.0 ** -0.5
EPS = 1e-5

QOFF = 0
KOFF = 16384
VOFF = 24576
BIGC = VOFF + 64 * 130

OSB = 4096

SG, SU, SD, BQ, BP = 0, 64, 128, 144, 162
ST0, ST1, ST2 = 0, 512, 1024

STAGE = "full"


class Prog:
    ENGS = ("tensor", "vector", "scalar", "gpsimd", "sync")

    def __init__(self, nc, es):
        self.nc = nc
        self.es = es
        self.q = {e: [] for e in self.ENGS}
        self.sems = {}
        self.cnt = {}

    def sem(self, name):
        self.sems[name] = self.es.enter_context(self.nc.semaphore(name))
        self.cnt[name] = 0

    def do(self, eng, fn, inc=None, n=1):
        tok = None
        if inc is not None:
            self.cnt[inc] += n
            tok = self.cnt[inc]
            sem = self.sems[inc]
            self.q[eng].append(lambda e, fn=fn, sem=sem, n=n: fn(e).then_inc(sem, n))
        else:
            self.q[eng].append(lambda e, fn=fn: fn(e))
        return tok

    def wait(self, eng, semname, tok):
        if tok is None or tok <= 0:
            return
        sem = self.sems[semname]
        self.q[eng].append(lambda e, sem=sem, tok=tok: e.wait_ge(sem, tok))

    def emit_block(self):
        nonempty = [e for e in self.ENGS if self.q[e]]
        if not nonempty:
            return
        with self.nc.Block() as block:
            for engname in nonempty:
                lst = self.q[engname]
                getattr(block, engname)(lambda e, lst=lst: [f(e) for f in lst])
        self.q = {e: [] for e in self.ENGS}


def build_nc():
    nc = bass.Bass("TRN2", target_bir_lowering=False)

    xTd = nc.dram_tensor("xT", [H, 1024], F32, kind="ExternalInput").ap()
    wg = nc.dram_tensor("wg", [MCG, 2, 128, 2048], I8, kind="ExternalInput").ap()
    wd = nc.dram_tensor("wd", [16, 4, 128, 2048], I8, kind="ExternalInput").ap()
    wq = nc.dram_tensor("wq", [18, 128, 2048], F32, kind="ExternalInput").ap()
    wp = nc.dram_tensor("wp", [16, 128, 2048], F32, kind="ExternalInput").ap()
    vecs = nc.dram_tensor("vecs", [128, 180], F32, kind="ExternalInput").ap()
    yTd = nc.dram_tensor("yT", [H, 1024], F32, kind="ExternalOutput").ap()
    ccin = nc.dram_tensor("ccin", [128, 2048], BF16, kind="Internal")
    ccout = nc.dram_tensor("ccout", [1024, 2048], BF16, kind="Internal", addr_space="Shared")

    es = contextlib.ExitStack()
    sb = lambda n, s, d: es.enter_context(nc.sbuf_tensor(n, s, d))
    ps = lambda n, s, d: es.enter_context(nc.psum_tensor(n, s, d))

    xT = sb("xTs", [128, KC * 1024], F32)
    xn = sb("xns", [128, KC * 512], BF16)
    big = sb("bigs", [128, BIGC], BF16)
    wsb = sb("wsbs", [128, 12288], BF16)
    gu = sb("gus", [128, 2048], F32)
    sbf = sb("sbfs", [128, 4096], BF16)
    svec = sb("svecs", [128, 180], F32)
    st = sb("sts", [1, 1536], F32)
    ones_bf = sb("onesb", [128, 1], BF16)
    ones_bf1 = sb("onesb1", [1, 128], BF16)
    ones_f1 = sb("onesf", [1, 128], F32)
    ident = sb("idents", [128, 128], BF16)

    pA = ps("pA", [128, 1024], F32)
    pB = ps("pB", [128, 1024], F32)
    pC = ps("pC", [128, 1024], F32)
    pD = ps("pD", [128, 512], F32)
    pT = ps("pT", [128, 1024], BF16)

    b = Prog(nc, es)
    for s in ("ts", "vs", "ss", "gs", "xl", "vl", "ccd", "ccs", "krb", "krv", "sto",
              "wg0", "wg1", "wu0", "wu1", "wd0", "wd1", "wq0", "wq1"):
        b.sem(s)
    for s in ("at_t", "at_v", "at_s", "at_g", "at_d0", "at_d1", "at_done"):
        b.sem(s)

    free = {}

    def wait_free(eng, key):
        ent = free.get(key)
        if ent is not None:
            b.wait(eng, ent[0], ent[1])

    b.do("sync", lambda e: e.dma_start(
        out=xT[:, :].rearrange("p (kc t) -> p kc t", kc=KC),
        in_=xTd.rearrange("(kc p) t -> p kc t", p=128)), inc="xl", n=16)
    b.do("sync", lambda e: e.dma_start(out=svec[:, :], in_=vecs), inc="vl", n=16)

    b.do("gpsimd", lambda e: e.memset(ones_bf[:], 1.0), inc="gs")
    b.do("gpsimd", lambda e: e.memset(ones_bf1[:], 1.0), inc="gs")
    tok_ones = b.do("gpsimd", lambda e: e.memset(ones_f1[:], 1.0), inc="gs")
    tok_identm = b.do("gpsimd", lambda e: e.memset(ident[:], 0.0), inc="gs")
    b.wait("gpsimd", "gs", tok_identm)
    tok_ident = b.do("gpsimd", lambda e: e.affine_select(
        out=ident[:], in_=ident[:], compare_op=ALU.not_equal, fill=1.0,
        base=0, pattern=[[-1, 128]], channel_multiplier=1), inc="gs")

    b.wait("vector", "vl", 16)
    b.do("vector", lambda e: e.tensor_scalar_mul(
        svec[:, SD:SD + 16], svec[:, SD:SD + 16], 0.5), inc="vs")
    b.wait("scalar", "vl", 16)
    b.wait("vector", "xl", 16)
    b.wait("scalar", "xl", 16)

    def emit_ln(blk, tok_xn_free):
        xc = lambda kc: kc * 1024 + blk * 512
        b.wait("tensor", "gs", 3)
        b.wait("vector", "ts", tok_xn_free)
        wait_free("vector", "st")
        wait_free("tensor", ("pAh", 0))
        wait_free("tensor", ("pAh", 1))
        tok_m = None
        for kc in range(KC):
            tok_c = b.do("vector", lambda e, kc=kc: e.tensor_copy(
                xn[:, kc * 512:(kc + 1) * 512], xT[:, xc(kc):xc(kc) + 512]),
                inc="vs")
            sl = (kc % 2) * 512
            if kc % 2 == 0:
                wait_free("scalar", ("sq", 0))
                tok_sq = b.do("scalar", lambda e, kc=kc, sl=sl: e.activation(
                    sbf[:, sl:sl + 512], xT[:, xc(kc):xc(kc) + 512], AF.Square),
                    inc="ss")
                sqsem = "ss"
            else:
                wait_free("vector", ("sq", 1))
                tok_sq = b.do("vector", lambda e, kc=kc, sl=sl: e.tensor_mul(
                    sbf[:, sl:sl + 512], xT[:, xc(kc):xc(kc) + 512],
                    xT[:, xc(kc):xc(kc) + 512]), inc="vs")
                sqsem = "vs"
            b.wait("tensor", "vs", tok_c)
            b.do("tensor", lambda e, kc=kc: e.matmul(
                pA[0:1, 0:512], ones_bf[:], xn[:, kc * 512:(kc + 1) * 512],
                start=(kc == 0), stop=(kc == KC - 1)))
            b.wait("tensor", sqsem, tok_sq)
            tok_m = b.do("tensor", lambda e, kc=kc, sl=sl: e.matmul(
                pA[0:1, 512:1024], ones_bf[:], sbf[:, sl:sl + 512],
                start=(kc == 0), stop=(kc == KC - 1)), inc="ts")
            free[("sq", kc % 2)] = ("ts", tok_m)
        b.wait("vector", "ts", tok_m)
        t1 = b.do("vector", lambda e: e.tensor_scalar_mul(
            st[0:1, ST0:ST0 + 512], pA[0:1, 0:512], 1.0 / H), inc="vs")
        t2 = b.do("vector", lambda e: e.tensor_scalar_mul(
            st[0:1, ST1:ST1 + 512], pA[0:1, 512:1024], 1.0 / H), inc="vs")
        free[("pAh", 0)] = ("vs", t1)
        free[("pAh", 1)] = ("vs", t2)
        b.wait("vector", "vs", t1)
        t3 = b.do("vector", lambda e: e.tensor_mul(
            st[0:1, ST2:ST2 + 512], st[0:1, ST0:ST0 + 512],
            st[0:1, ST0:ST0 + 512]), inc="vs")
        b.wait("vector", "vs", t3)
        t4 = b.do("vector", lambda e: e.tensor_sub(
            st[0:1, ST1:ST1 + 512], st[0:1, ST1:ST1 + 512],
            st[0:1, ST2:ST2 + 512]), inc="vs")
        b.wait("scalar", "vs", t4)
        t5 = b.do("scalar", lambda e: e.activation(
            st[0:1, ST2:ST2 + 512], st[0:1, ST1:ST1 + 512], AF.Sqrt,
            bias=svec[0:1, 178:179]),
            inc="ss")
        b.wait("vector", "ss", t5)

        def _recip_bf(e):
            with nc.allow_low_precision(reason="LN 1/std rounded to bf16"):
                return e.reciprocal(sbf[0:1, 0:512], st[0:1, ST2:ST2 + 512])
        t6 = b.do("vector", _recip_bf, inc="vs")
        b.wait("vector", "vs", t6)
        t7 = b.do("vector", lambda e: e.tensor_mul(
            sbf[0:1, 512:1024], st[0:1, ST0:ST0 + 512],
            sbf[0:1, 0:512]), inc="vs")
        wait_free("tensor", ("pBh", 0))
        wait_free("tensor", ("pBh", 1))
        b.wait("tensor", "vs", t7)
        b.do("tensor", lambda e: e.matmul(
            pB[:, 0:512], ones_bf1[:], sbf[0:1, 0:512],
            start=True, stop=True))
        tok_bc = b.do("tensor", lambda e: e.matmul(
            pB[:, 512:1024], ones_bf1[:], sbf[0:1, 512:1024],
            start=True, stop=True), inc="ts")
        free["st"] = ("ts", tok_bc)
        b.wait("vector", "ts", tok_bc)
        c1 = b.do("vector", lambda e: e.tensor_copy(
            sbf[:, 0:512], pB[:, 0:512]), inc="vs")
        c2 = b.do("vector", lambda e: e.tensor_copy(
            sbf[:, 512:1024], pB[:, 512:1024]), inc="vs")
        free[("pBh", 0)] = ("vs", c2)
        free[("pBh", 1)] = ("vs", c2)
        b.wait("vector", "vs", c2)
        tok_aps = []
        for kc in range(KC):
            ta = b.do("vector", lambda e, kc=kc: e.tensor_mul(
                xn[:, kc * 512:(kc + 1) * 512],
                xn[:, kc * 512:(kc + 1) * 512], sbf[:, 0:512]), inc="vs")
            b.wait("vector", "vs", ta)
            tok_aps.append(b.do("vector", lambda e, kc=kc: e.tensor_sub(
                xn[:, kc * 512:(kc + 1) * 512],
                xn[:, kc * 512:(kc + 1) * 512], sbf[:, 512:1024]), inc="vs"))
        free[("sq", 0)] = ("vs", tok_aps[-1])
        free[("sq", 1)] = ("vs", tok_aps[-1])
        return tok_aps

    def emit_ffn_block(blk, tok_xn):
        tok_xn_list = tok_xn if isinstance(tok_xn, list) else None
        tok_fuse_last = None
        tok_gu_mm_last = None
        for m in range(MCG):
            p = m % 2
            gsl, usl = p * 4096, p * 4096 + 2048
            wait_free("gpsimd", ("gslab", p))
            wait_free("gpsimd", ("uslab", p))
            kg = f"wg{p}"
            tg = b.do("gpsimd", lambda e, m=m, gsl=gsl: e.dma_start(
                out=wsb[:, gsl:gsl + 4096].rearrange("p (g c) -> p g c", g=2),
                in_=wg[m].rearrange("g p c -> p g c")), inc=kg, n=16)
            tu = tg
            ku = kg
            b.wait("tensor", kg, tg)
            if m == 0:
                wait_free("tensor", "fuse")
            wait_free("tensor", ("pAh", p))
            for k in range(KC):
                if m == 0 and tok_xn_list is not None:
                    b.wait("tensor", "vs", tok_xn_list[k])
                elif m == 0 and k == 0:
                    b.wait("tensor", "vs", tok_xn)
                tok = b.do("tensor", lambda e, k=k, p=p, gsl=gsl: e.matmul(
                    pA[:, p * 512:(p + 1) * 512],
                    wsb[:, gsl + k * 128:gsl + (k + 1) * 128],
                    xn[:, k * 512:(k + 1) * 512],
                    start=(k == 0), stop=(k == KC - 1)),
                    inc="ts" if k == KC - 1 else None)
            tok_gmm = tok
            free[("gslab", p)] = ("ts", tok_gmm)
            b.wait("tensor", ku, tu)
            wait_free("tensor", ("pBh", p))
            for k in range(KC):
                tok = b.do("tensor", lambda e, k=k, p=p, usl=usl: e.matmul(
                    pB[:, p * 512:(p + 1) * 512],
                    wsb[:, usl + k * 128:usl + (k + 1) * 128],
                    xn[:, k * 512:(k + 1) * 512],
                    start=(k == 0), stop=(k == KC - 1)),
                    inc="ts" if k == KC - 1 else None)
            tok_umm = tok
            tok_gu_mm_last = tok_umm
            free[("uslab", p)] = ("ts", tok_umm)
            gsf, usf = (p * 2) * 512, (p * 2 + 1) * 512
            b.wait("scalar", "ts", tok_gmm)
            wait_free("scalar", ("gst", p))
            tok_si = b.do("scalar", lambda e, m=m, p=p, gsf=gsf: e.activation(
                gu[:, gsf:gsf + 512], pA[:, p * 512:(p + 1) * 512], AF.Silu,
                scale=svec[:, SG + m:SG + m + 1]), inc="ss")
            free[("pAh", p)] = ("ss", tok_si)
            b.wait("vector", "ts", tok_umm)
            wait_free("vector", ("gst", p))
            tok_ue = b.do("vector", lambda e, m=m, p=p, usf=usf: e.tensor_scalar_mul(
                gu[:, usf:usf + 512], pB[:, p * 512:(p + 1) * 512],
                svec[:, SU + m:SU + m + 1]), inc="vs")
            free[("pBh", p)] = ("vs", tok_ue)
            b.wait("vector", "ss", tok_si)
            b.wait("vector", "vs", tok_ue)
            tok_fuse_last = b.do("vector", lambda e, m=m, gsf=gsf, usf=usf: e.tensor_mul(
                big[:, m * 512:(m + 1) * 512], gu[:, gsf:gsf + 512],
                gu[:, usf:usf + 512]), inc="vs")
            free[("gst", p)] = ("vs", tok_fuse_last)
        tok_mm_d = None
        tok_devac = None
        for mo in range(16):
            q = mo % 2
            for sub in range(4):
                ss2 = (mo * 4 + sub) % 2
                dsl = 8192 + ss2 * 2048
                kd = f"wd{ss2}"
                wait_free("gpsimd", ("dslab", ss2))
                td = b.do("gpsimd", lambda e, mo=mo, sub=sub, dsl=dsl: e.dma_start(
                    out=wsb[:, dsl:dsl + 2048], in_=wd[mo, sub]), inc=kd, n=16)
                b.wait("tensor", kd, td)
                if mo == 0 and sub == 0:
                    b.wait("tensor", "vs", tok_fuse_last)
                if sub == 0:
                    wait_free("tensor", ("pCh", q))
                for kk in range(16):
                    k = sub * 16 + kk
                    tok_mm_d = b.do("tensor", lambda e, k=k, kk=kk, q=q, dsl=dsl: e.matmul(
                        pC[:, q * 512:(q + 1) * 512],
                        wsb[:, dsl + kk * 128:dsl + (kk + 1) * 128],
                        big[:, k * 512:(k + 1) * 512],
                        start=(k == 0), stop=(k == MCG - 1)),
                        inc="ts" if kk == 15 else None)
                free[("dslab", ss2)] = ("ts", tok_mm_d)
            b.wait("vector", "ts", tok_mm_d)
            xcol = mo * 1024 + blk * 512
            tok_devac = b.do("vector", lambda e, q=q, xcol=xcol, mo=mo: e.scalar_tensor_tensor(
                xT[:, xcol:xcol + 512], pC[:, q * 512:(q + 1) * 512],
                svec[:, SD + mo:SD + mo + 1], xT[:, xcol:xcol + 512],
                ALU.mult, ALU.add), inc="vs")
            free[("pCh", q)] = ("vs", tok_devac)
        free["fuse"] = ("ts", tok_mm_d)
        return tok_devac, tok_mm_d, tok_gu_mm_last

    tok_xn_free = None
    tok_ffn_evac = tok_ffn_mm = None
    for blk in (0, 1):
        tok_xn = emit_ln(blk, tok_xn_free)
        tok_ffn_evac, tok_ffn_mm, tok_xn_free = emit_ffn_block(blk, tok_xn)
    tok_ffn1_mm = tok_ffn_mm

    def emit_store(waits):
        for s, t in waits:
            b.wait("sync", s, t)
        b.do("sync", lambda e: e.dma_start(
            out=yTd.rearrange("(kc p) t -> p kc t", p=128),
            in_=xT[:, :].rearrange("p (kc t) -> p kc t", kc=KC)), inc="sto", n=16)
        b.wait("sync", "sto", 16)

    if STAGE == "ffn1":
        emit_store([("vs", tok_ffn_evac)])
        b.emit_block()
        es.close()
        return nc

    tok_qev = {}
    tok_qkv_mm = None
    qi = 0

    def emit_qkv_m(blk, m, tok_xn2, chase=False):
        nonlocal tok_qkv_mm, qi
        p = qi % 2
        qi += 1
        qsl = p * 2048
        kq = f"wq{p}"
        if qi <= 2:
            b.wait("gpsimd", "ts", tok_ffn1_mm)
        wait_free("gpsimd", ("qslab", p))
        tq = b.do("gpsimd", lambda e, m=m, qsl=qsl: e.dma_start(
            out=wsb[:, qsl:qsl + 2048], in_=wq[m]), inc=kq, n=16)
        b.wait("tensor", kq, tq)
        if not chase:
            b.wait("tensor", "vs",
                   tok_xn2[-1] if isinstance(tok_xn2, list) else tok_xn2)
        wait_free("tensor", ("pCh", p))
        for k in range(KC):
            if chase and isinstance(tok_xn2, list):
                b.wait("tensor", "vs", tok_xn2[k])
            tok_qkv_mm = b.do("tensor", lambda e, k=k, p=p, qsl=qsl: e.matmul(
                pC[:, p * 512:(p + 1) * 512],
                wsb[:, qsl + k * 128:qsl + (k + 1) * 128],
                xn[:, k * 512:(k + 1) * 512],
                start=(k == 0), stop=(k == KC - 1)),
                inc="ts" if k == KC - 1 else None)
        free[("qslab", p)] = ("ts", tok_qkv_mm)
        if m < 16:
            darr, dcol = big, m * 1024 + blk * 512
        elif m == 16:
            darr, dcol = sbf, 1024 + blk * 512
        else:
            darr, dcol = sbf, 3072 + blk * 512
        b.wait("scalar", "ts", tok_qkv_mm)
        tok_ev = b.do("scalar", lambda e, m=m, p=p, darr=darr, dcol=dcol: e.activation(
            darr[:, dcol:dcol + 512], pC[:, p * 512:(p + 1) * 512],
            AF.Identity, bias=svec[:, BQ + m:BQ + m + 1]), inc="ss")
        free[("pCh", p)] = ("ss", tok_ev)
        tok_qev[(blk, m)] = tok_ev

    tok_xn2_blk = {}
    tok_xn2_blk[0] = emit_ln(0, tok_xn_free)
    first = True
    for m in (16, 17) + tuple(range(16)):
        emit_qkv_m(0, m, tok_xn2_blk[0], chase=first)
        first = False
    tok_xn_free = tok_qkv_mm
    tok_xn2_blk[1] = emit_ln(1, tok_xn_free)
    first = True
    for m in (16, 17):
        emit_qkv_m(1, m, tok_xn2_blk[1], chase=first)
        first = False

    b.wait("tensor", "gs", tok_ident)
    b.wait("tensor", "ss", tok_qev[(0, 17)])
    b.wait("tensor", "ss", tok_qev[(1, 17)])
    tok_tp = None
    for j in range(8):
        tok_tp = b.do("tensor", lambda e, j=j: e.transpose(
            pT[:, j * 128:(j + 1) * 128],
            sbf[:, 3072 + j * 128:3072 + (j + 1) * 128], ident[:]), inc="ts")
    b.wait("vector", "ts", tok_tp)
    tok_vtc = b.do("vector", lambda e: e.tensor_copy(
        sbf[:, 2048:3072], pT[:, 0:1024]), inc="vs")
    b.wait("sync", "ss", tok_qev[(1, 16)])
    b.wait("sync", "ss", tok_qev[(0, 16)])
    b.wait("sync", "vs", tok_vtc)
    b.do("sync", lambda e: e.dma_start(out=ccin.ap(), in_=sbf[:, 1024:3072]),
         inc="ccd", n=16)

    for m in range(16):
        emit_qkv_m(1, m, tok_xn2_blk[1])
    tok_xn_free = tok_qkv_mm

    b.wait("gpsimd", "ccd", 16)
    b.do("gpsimd", lambda e: e.collective_compute(
        "AllGather", ALU.bypass, replica_groups=[list(range(NC))],
        ins=[ccin.ap().opt()], outs=[ccout.ap().opt()]), inc="ccs")
    b.wait("gpsimd", "ts", tok_ffn1_mm)
    tok_vmset = b.do("gpsimd", lambda e: e.memset(big[:, VOFF:BIGC], 1.0),
                     inc="gs")

    b.wait("gpsimd", "ts", tok_qkv_mm)
    tok_masks = None
    for d in range(4):
        tok_ms = b.do("gpsimd", lambda e, d=d: e.memset(
            xn[:, d * 512:(d + 1) * 512], 1.0), inc="gs")
        b.wait("gpsimd", "gs", tok_ms)
        tok_masks = b.do("gpsimd", lambda e, d=d: e.affine_select(
            out=xn[:, d * 512:(d + 1) * 512], in_=xn[:, d * 512:(d + 1) * 512],
            compare_op=ALU.is_ge, fill=0.0, base=-(128 * d),
            pattern=[[1, 512]], channel_multiplier=-1), inc="gs")

    b.wait("sync", "ccs", 1)
    b.wait("sync", "ts", tok_ffn1_mm)
    b.wait("sync", "gs", tok_vmset)
    co = ccout.ap()
    for r in range(NC):
        for jb, gb in ((0, r), (1, 15 - r)):
            b.do("sync", lambda e, r=r, jb=jb, gb=gb: e.dma_start(
                out=big[:, KOFF + gb * 512:KOFF + gb * 512 + 512],
                in_=co[r * 128:(r + 1) * 128, jb * 512:(jb + 1) * 512]),
                inc="krb", n=16)
    for r in range(NC):
        for jb, gb in ((0, r), (1, 15 - r)):
            for hf in range(2):
                b.do("sync", lambda e, r=r, jb=jb, gb=gb, hf=hf: e.dma_start(
                    out=big[:, VOFF + 4 * gb * 130:VOFF + 4 * gb * 130 + 520]
                        .rearrange("p (j r) -> p j r", r=130)
                        [:, :, hf * 65:hf * 65 + 64],
                    in_=co[r * 128:(r + 1) * 128,
                           1024 + jb * 512:1024 + (jb + 1) * 512]
                        .rearrange("p (j r) -> p j r", r=128)
                        [:, :, hf * 64:hf * 64 + 64]),
                    inc="krv", n=16)
    pre_tot = {s: b.cnt[s] for s in ("ts", "vs", "ss", "gs", "krb", "ccd")}
    krv_total = b.cnt["krv"]

    b.emit_block()

    sems = b.sems
    engines = [nc.tensor, nc.scalar, nc.vector, nc.gpsimd]
    idx = [nc.partition_id(engines=[ET.PE]),
           nc.partition_id(engines=[ET.Activation]),
           nc.partition_id(engines=[ET.DVE]),
           nc.partition_id(engines=[ET.Pool])]

    for body in nc.Switch(engines=engines, index=idx, n=NC):
        emit_attention_body(nc, sems, body, pre_tot, krv_total, xT, xn,
                            big, wsb, gu, sbf, st, svec, ones_bf1,
                            pA, pB, pC, pD, pT, wp)

    if STAGE == "attn":
        b.wait("sync", "at_done", 4)
        emit_store([])
        b.emit_block()
        es.close()
        return nc

    for key in [("pAh", 0), ("pAh", 1), ("pBh", 0), ("pBh", 1),
                ("pCh", 0), ("pCh", 1), "fuse", "st",
                ("gslab", 0), ("gslab", 1), ("uslab", 0), ("uslab", 1),
                ("dslab", 0), ("dslab", 1), ("sq", 0), ("sq", 1),
                ("gst", 0), ("gst", 1)]:
        free[key] = None
    tok_xn_free = tok_qkv_mm
    for blk in (0, 1):
        tok_xn = emit_ln(blk, tok_xn_free)
        tok_ffn_evac, tok_ffn_mm, tok_xn_free = emit_ffn_block(blk, tok_xn)

    emit_store([("vs", tok_ffn_evac)])
    b.emit_block()
    es.close()
    return nc


def emit_attention_body(nc, sems, c, pre_tot, krv_total, xT, xn,
                        big, wsb, gu, sbf, st, svec, ones_bf1,
                        pA, pB, pC, pD, pT, wp):
    at = {k: 0 for k in ("t", "v", "s", "g", "d0", "d1")}
    S_ = {"t": sems["at_t"], "v": sems["at_v"], "s": sems["at_s"],
          "g": sems["at_g"], "d0": sems["at_d0"], "d1": sems["at_d1"]}
    E = {"t": nc.tensor, "v": nc.vector, "s": nc.scalar, "g": nc.gpsimd}

    def inc(key, ins, n=1):
        at[key] += n
        ins.then_inc(S_[key], n)
        return at[key]

    def w(ekey, skey, tok):
        if tok and tok > 0:
            E[ekey].wait_ge(S_[skey], tok)

    for ekey in ("t", "s", "v", "g"):
        for s, v in pre_tot.items():
            if v > 0:
                E[ekey].wait_ge(sems[s], v)
    st_body = {"krv_waited": False}

    fre = {}

    def wfree(ekey, key):
        ent = fre.get(key)
        if ent is not None:
            w(ekey, ent[0], ent[1])

    pending = []
    for qbl, gb in ((0, c), (1, 15 - c)):
        nk = 4 * (gb + 1)
        for h in range(NH):
            qh = big[:, h * 1024 + qbl * 512: h * 1024 + qbl * 512 + 512]
            pvb = pB if h % 2 == 0 else pC
            pvname = ("pv", h % 2)

            def sc_ap(kt):
                r = kt % 3
                return pD[:, 0:512] if r == 2 else pA[:, r * 512:(r + 1) * 512]

            def scores(kt):
                wfree("t", ("scr", kt % 3))
                ins = nc.tensor.matmul(
                    sc_ap(kt), big[:, KOFF + kt * 128:KOFF + (kt + 1) * 128],
                    qh, start=True, stop=True)
                return inc("t", ins)

            tok_sc = {0: scores(0), 1: scores(1)}
            while pending:
                pending.pop(0)()
            tok_exp = {}
            tok_mask = {}
            tok_pv = None

            def emit_pv(kt):
                nonlocal tok_pv
                sl = kt % 4
                if not st_body["krv_waited"]:
                    nc.tensor.wait_ge(sems["krv"], krv_total)
                    st_body["krv_waited"] = True
                w("t", "s", tok_exp[kt])
                if kt in tok_mask:
                    w("t", "v", tok_mask[kt])
                if kt == 0:
                    wfree("t", pvname)
                va = VOFF + kt * 130
                nc.tensor.matmul(pvb[0:65, 0:512], big[:, va:va + 65],
                                 sbf[:, sl * 512:(sl + 1) * 512],
                                 start=(kt == 0), stop=(kt == nk - 1))
                ins = nc.tensor.matmul(pvb[0:65, 512:1024],
                                       big[:, va + 65:va + 130],
                                       sbf[:, sl * 512:(sl + 1) * 512],
                                       start=(kt == 0), stop=(kt == nk - 1))
                tok_pv = inc("t", ins)
                fre[("probs", sl)] = ("t", tok_pv)

            for kt in range(nk):
                sl = kt % 4
                w("s", "t", tok_sc[kt])
                wfree("s", ("probs", sl))
                ins = nc.scalar.activation(
                    sbf[:, sl * 512:(sl + 1) * 512],
                    sc_ap(kt), AF.Exp, scale=SCALE)
                tok_exp[kt] = inc("s", ins)
                fre[("scr", kt % 3)] = ("s", tok_exp[kt])
                if kt >= nk - 4:
                    d = kt - (nk - 4)
                    w("v", "s", tok_exp[kt])
                    ins = nc.vector.tensor_mul(
                        sbf[:, sl * 512:(sl + 1) * 512],
                        sbf[:, sl * 512:(sl + 1) * 512],
                        xn[:, d * 512:(d + 1) * 512])
                    tok_mask[kt] = inc("v", ins)
                if kt + 2 < nk:
                    tok_sc[kt + 2] = scores(kt + 2)
                if kt >= 1:
                    emit_pv(kt - 1)
            emit_pv(nk - 1)
            w("v", "t", tok_pv)
            wfree("v", ("strow", h % 2))
            rrow = sbf[0:1, 2048 + (h % 2) * 512:2048 + (h % 2) * 512 + 512]
            with nc.allow_low_precision(reason="softmax 1/sum rounded to bf16"):
                ins = nc.vector.reciprocal(rrow, pvb[64:65, 0:512])
            tok_rec = inc("v", ins)

            def flush_norm(h=h, pvb=pvb, pvname=pvname, tok_rec=tok_rec,
                           rrow=rrow):
                w("t", "v", tok_rec)
                wfree("t", ("scr", 2))
                ins = nc.tensor.matmul(pD[:, 0:512], ones_bf1[:], rrow,
                                       start=True, stop=True)
                tok_bc = inc("t", ins)
                fre[("strow", h % 2)] = ("t", tok_bc)
                w("v", "t", tok_bc)
                ins = nc.vector.tensor_copy(gu[:, 0:512], pD[:, 0:512])
                tok_rb = inc("v", ins)
                fre[("scr", 2)] = ("v", tok_rb)
                oc = OSB + h * 512
                w("v", "v", tok_rb)
                ins = nc.vector.tensor_mul(
                    wsb[0:64, oc:oc + 512], pvb[0:64, 0:512], gu[0:64, 0:512])
                inc("v", ins)
                ins = nc.vector.tensor_mul(
                    wsb[64:128, oc:oc + 512], pvb[0:64, 512:1024],
                    gu[64:128, 0:512])
                tok_norm = inc("v", ins)
                fre[pvname] = ("v", tok_norm)
            pending.append(flush_norm)
        while pending:
            pending.pop(0)()
        for m in range(16):
            p = m % 2
            dkey = f"d{p}"
            wfree("g", ("pslab", p))
            ins = nc.gpsimd.dma_start(out=wsb[:, p * 2048:(p + 1) * 2048],
                                      in_=wp[m])
            tok_w = inc(dkey, ins, 16)
            w("t", dkey, tok_w)
            if m == 0:
                w("t", "s", at["s"])
                w("t", "v", at["v"])
            wfree("t", ("pAp", p))
            for k in range(KC):
                ins = nc.tensor.matmul(
                    pA[:, p * 512:(p + 1) * 512],
                    wsb[:, p * 2048 + k * 128:p * 2048 + (k + 1) * 128],
                    wsb[:, OSB + k * 512:OSB + (k + 1) * 512],
                    start=(k == 0), stop=(k == KC - 1))
            tok_pm = inc("t", ins)
            fre[("pslab", p)] = ("t", tok_pm)
            w("v", "t", tok_pm)
            xcol = m * 1024 + qbl * 512
            ins = nc.vector.scalar_tensor_tensor(
                xT[:, xcol:xcol + 512],
                pA[:, p * 512:(p + 1) * 512],
                svec[:, BP + m:BP + m + 1],
                xT[:, xcol:xcol + 512],
                ALU.add, ALU.add)
            tok_pe = inc("v", ins)
            fre[("pAp", p)] = ("v", tok_pe)
        fre[("scr", 0)] = ("v", at["v"])
        fre[("scr", 1)] = ("v", at["v"])

    nc.tensor.wait_ge(S_["t"], at["t"])
    nc.scalar.wait_ge(S_["s"], at["s"])
    nc.vector.wait_ge(S_["v"], at["v"])
    nc.gpsimd.wait_ge(S_["g"], at["g"])
    for eng in (nc.tensor, nc.scalar, nc.vector, nc.gpsimd):
        eng.sem_inc(sems["at_done"], 1)
    for eng in (nc.tensor, nc.scalar, nc.vector, nc.gpsimd):
        eng.wait_ge(sems["at_done"], 4)


def _shard(inputs):
    x = np.asarray(inputs["L0_attn_x"], dtype=np.float32).reshape(8192, H)
    xT = np.ascontiguousarray(x.T)
    wgate = np.asarray(inputs["L0_ffn_w_gate"])
    wup = np.asarray(inputs["L0_ffn_w_up"])
    wdown = np.asarray(inputs["L0_ffn_w_down"])
    Wq = np.asarray(inputs["L0_attn_W_qkv"], dtype=np.float32)
    Wp = np.asarray(inputs["L0_attn_W_proj"], dtype=np.float32)

    wg_only = wgate.reshape(16, 128, 64, 128).transpose(2, 1, 0, 3).reshape(64, 1, 128, 2048)
    wu_only = wup.reshape(16, 128, 64, 128).transpose(2, 1, 0, 3).reshape(64, 1, 128, 2048)
    wg_t = np.ascontiguousarray(np.concatenate([wg_only, wu_only], axis=1))
    wd_t = np.ascontiguousarray(
        wdown.reshape(64, 128, 16, 128).transpose(2, 1, 0, 3)
        .reshape(16, 128, 4, 2048).transpose(0, 2, 1, 3))
    wq_t = np.ascontiguousarray(
        Wq.reshape(16, 128, 18, 128).transpose(2, 1, 0, 3).reshape(18, 128, 2048))
    wp_t = np.ascontiguousarray(
        Wp.reshape(16, 128, 16, 128).transpose(2, 1, 0, 3).reshape(16, 128, 2048))

    vec = np.zeros((128, 180), dtype=np.float32)
    vec[:, SG:SG + 64] = np.asarray(inputs["L0_ffn_s_gate"],
                                    dtype=np.float32).reshape(64, 128).T
    vec[:, SU:SU + 64] = np.asarray(inputs["L0_ffn_s_up"],
                                    dtype=np.float32).reshape(64, 128).T
    vec[:, SD:SD + 16] = np.asarray(inputs["L0_ffn_s_down"],
                                    dtype=np.float32).reshape(16, 128).T
    vec[:, BQ:BQ + 18] = np.asarray(inputs["L0_attn_b_qkv"],
                                    dtype=np.float32).reshape(18, 128).T
    vec[:, BP:BP + 16] = np.asarray(inputs["L0_attn_b_proj"],
                                    dtype=np.float32).reshape(16, 128).T
    vec[:, 178] = EPS

    in_maps = []
    for c in range(NC):
        xTc = np.ascontiguousarray(np.concatenate(
            [xT[:, 512 * c:512 * (c + 1)],
             xT[:, 512 * (15 - c):512 * (16 - c)]], axis=1))
        in_maps.append(dict(xT=xTc, wg=wg_t, wd=wd_t, wq=wq_t,
                            wp=wp_t, vecs=vec))
    return in_maps


_NC_CACHE = {}


def _tune_compiler_flags():
    try:
        from concourse.compiler_utils import get_compiler_flags, set_compiler_flags
        flags = get_compiler_flags()
        out = []
        for f in flags:
            if f.startswith("--internal-backend-options="):
                f = f.replace("--enable-ldw-opt=false", "--enable-ldw-opt=true")
            out.append(f)
        set_compiler_flags(out)
    except Exception:
        pass


def kernel(**inputs):
    _tune_compiler_flags()
    in_maps = _shard(inputs)
    if STAGE not in _NC_CACHE:
        _NC_CACHE[STAGE] = build_nc()
    nc = _NC_CACHE[STAGE]
    res = run_bass_kernel_spmd(nc, in_maps, core_ids=list(range(NC)))
    yT = np.empty((H, 8192), dtype=np.float32)
    for c in range(NC):
        yc = res.results[c]["yT"]
        yT[:, 512 * c:512 * (c + 1)] = yc[:, 0:512]
        yT[:, 512 * (15 - c):512 * (16 - c)] = yc[:, 512:1024]
    return np.ascontiguousarray(yT.T).reshape(1, 8192, H)


# version 13
# speedup vs baseline: 1.0063x; 1.0063x over previous
import contextlib
import numpy as np
import ml_dtypes
import concourse.bass as bass
import concourse.mybir as mybir
from concourse.bass_utils import run_bass_kernel_spmd

F32 = mybir.dt.float32
BF16 = mybir.dt.bfloat16
I8 = mybir.dt.int8
AF = mybir.ActivationFunctionType
ALU = mybir.AluOpType
ET = mybir.EngineType

H = 2048
NH = 16
INTER = 8192
NC = 8
KC = 16
MCG = 64
SCALE = 128.0 ** -0.5
EPS = 1e-5

QOFF = 0
KOFF = 16384
VOFF = 24576
BIGC = VOFF + 64 * 130

OSB = 4096

SG, SU, SD, BQ, BP = 0, 64, 128, 144, 162
ST0, ST1, ST2 = 0, 512, 1024

STAGE = "full"


class Prog:
    ENGS = ("tensor", "vector", "scalar", "gpsimd", "sync")

    def __init__(self, nc, es):
        self.nc = nc
        self.es = es
        self.q = {e: [] for e in self.ENGS}
        self.sems = {}
        self.cnt = {}

    def sem(self, name):
        self.sems[name] = self.es.enter_context(self.nc.semaphore(name))
        self.cnt[name] = 0

    def do(self, eng, fn, inc=None, n=1):
        tok = None
        if inc is not None:
            self.cnt[inc] += n
            tok = self.cnt[inc]
            sem = self.sems[inc]
            self.q[eng].append(lambda e, fn=fn, sem=sem, n=n: fn(e).then_inc(sem, n))
        else:
            self.q[eng].append(lambda e, fn=fn: fn(e))
        return tok

    def wait(self, eng, semname, tok):
        if tok is None or tok <= 0:
            return
        sem = self.sems[semname]
        self.q[eng].append(lambda e, sem=sem, tok=tok: e.wait_ge(sem, tok))

    def emit_block(self):
        nonempty = [e for e in self.ENGS if self.q[e]]
        if not nonempty:
            return
        with self.nc.Block() as block:
            for engname in nonempty:
                lst = self.q[engname]
                getattr(block, engname)(lambda e, lst=lst: [f(e) for f in lst])
        self.q = {e: [] for e in self.ENGS}


def build_nc():
    nc = bass.Bass("TRN2", target_bir_lowering=False)

    xTd = nc.dram_tensor("xT", [H, 1024], F32, kind="ExternalInput").ap()
    wg = nc.dram_tensor("wg", [MCG, 2, 128, 2048], I8, kind="ExternalInput").ap()
    wd = nc.dram_tensor("wd", [16, 4, 128, 2048], I8, kind="ExternalInput").ap()
    wq = nc.dram_tensor("wq", [18, 128, 2048], F32, kind="ExternalInput").ap()
    wp = nc.dram_tensor("wp", [16, 128, 2048], F32, kind="ExternalInput").ap()
    vecs = nc.dram_tensor("vecs", [128, 180], F32, kind="ExternalInput").ap()
    yTd = nc.dram_tensor("yT", [H, 1024], F32, kind="ExternalOutput").ap()
    ccin = nc.dram_tensor("ccin", [128, 2048], BF16, kind="Internal")
    ccout = nc.dram_tensor("ccout", [1024, 2048], BF16, kind="Internal", addr_space="Shared")

    es = contextlib.ExitStack()
    sb = lambda n, s, d: es.enter_context(nc.sbuf_tensor(n, s, d))
    ps = lambda n, s, d: es.enter_context(nc.psum_tensor(n, s, d))

    xT = sb("xTs", [128, KC * 1024], F32)
    xn = sb("xns", [128, KC * 512], BF16)
    big = sb("bigs", [128, BIGC], BF16)
    wsb = sb("wsbs", [128, 12288], BF16)
    gu = sb("gus", [128, 2048], F32)
    sbf = sb("sbfs", [128, 4096], BF16)
    svec = sb("svecs", [128, 180], F32)
    st = sb("sts", [1, 1536], F32)
    ones_bf = sb("onesb", [128, 1], BF16)
    ones_bf1 = sb("onesb1", [1, 128], BF16)
    ones_f1 = sb("onesf", [1, 128], F32)
    ident = sb("idents", [128, 128], BF16)

    pA = ps("pA", [128, 1024], F32)
    pB = ps("pB", [128, 1024], F32)
    pC = ps("pC", [128, 1024], F32)
    pD = ps("pD", [128, 512], F32)
    pT = ps("pT", [128, 1024], BF16)

    b = Prog(nc, es)
    for s in ("ts", "vs", "ss", "gs", "xl", "vl", "ccd", "ccs", "krb", "krv", "sto",
              "wg0", "wg1", "wu0", "wu1", "wd0", "wd1", "wq0", "wq1"):
        b.sem(s)
    for s in ("at_t", "at_v", "at_s", "at_g", "at_d0", "at_d1", "at_done"):
        b.sem(s)

    free = {}

    def wait_free(eng, key):
        ent = free.get(key)
        if ent is not None:
            b.wait(eng, ent[0], ent[1])

    b.do("sync", lambda e: e.dma_start(
        out=xT[:, :].rearrange("p (kc t) -> p kc t", kc=KC),
        in_=xTd.rearrange("(kc p) t -> p kc t", p=128)), inc="xl", n=16)
    b.do("sync", lambda e: e.dma_start(out=svec[:, :], in_=vecs), inc="vl", n=16)

    b.do("gpsimd", lambda e: e.memset(ones_bf[:], 1.0), inc="gs")
    b.do("gpsimd", lambda e: e.memset(ones_bf1[:], 1.0), inc="gs")
    tok_ones = b.do("gpsimd", lambda e: e.memset(ones_f1[:], 1.0), inc="gs")
    tok_identm = b.do("gpsimd", lambda e: e.memset(ident[:], 0.0), inc="gs")
    b.wait("gpsimd", "gs", tok_identm)
    tok_ident = b.do("gpsimd", lambda e: e.affine_select(
        out=ident[:], in_=ident[:], compare_op=ALU.not_equal, fill=1.0,
        base=0, pattern=[[-1, 128]], channel_multiplier=1), inc="gs")

    b.wait("vector", "vl", 16)
    b.do("vector", lambda e: e.tensor_scalar_mul(
        svec[:, SD:SD + 16], svec[:, SD:SD + 16], 0.5), inc="vs")
    b.wait("scalar", "vl", 16)
    b.wait("vector", "xl", 16)
    b.wait("scalar", "xl", 16)

    def emit_ln(blk, tok_xn_free):
        xc = lambda kc: kc * 1024 + blk * 512
        b.wait("tensor", "gs", 3)
        b.wait("vector", "ts", tok_xn_free)
        wait_free("vector", "st")
        wait_free("tensor", ("pAh", 0))
        wait_free("tensor", ("pAh", 1))
        tok_m = None
        for kc in range(KC):
            tok_c = b.do("vector", lambda e, kc=kc: e.tensor_copy(
                xn[:, kc * 512:(kc + 1) * 512], xT[:, xc(kc):xc(kc) + 512]),
                inc="vs")
            sl = (kc % 2) * 512
            if kc % 2 == 0:
                wait_free("scalar", ("sq", 0))
                tok_sq = b.do("scalar", lambda e, kc=kc, sl=sl: e.activation(
                    sbf[:, sl:sl + 512], xT[:, xc(kc):xc(kc) + 512], AF.Square),
                    inc="ss")
                sqsem = "ss"
            else:
                wait_free("vector", ("sq", 1))
                tok_sq = b.do("vector", lambda e, kc=kc, sl=sl: e.tensor_mul(
                    sbf[:, sl:sl + 512], xT[:, xc(kc):xc(kc) + 512],
                    xT[:, xc(kc):xc(kc) + 512]), inc="vs")
                sqsem = "vs"
            b.wait("tensor", "vs", tok_c)
            b.do("tensor", lambda e, kc=kc: e.matmul(
                pA[0:1, 0:512], ones_bf[:], xn[:, kc * 512:(kc + 1) * 512],
                start=(kc == 0), stop=(kc == KC - 1)))
            b.wait("tensor", sqsem, tok_sq)
            tok_m = b.do("tensor", lambda e, kc=kc, sl=sl: e.matmul(
                pA[0:1, 512:1024], ones_bf[:], sbf[:, sl:sl + 512],
                start=(kc == 0), stop=(kc == KC - 1)), inc="ts")
            free[("sq", kc % 2)] = ("ts", tok_m)
        b.wait("vector", "ts", tok_m)
        t1 = b.do("vector", lambda e: e.tensor_scalar_mul(
            st[0:1, ST0:ST0 + 512], pA[0:1, 0:512], 1.0 / H), inc="vs")
        t2 = b.do("vector", lambda e: e.tensor_scalar_mul(
            st[0:1, ST1:ST1 + 512], pA[0:1, 512:1024], 1.0 / H), inc="vs")
        free[("pAh", 0)] = ("vs", t1)
        free[("pAh", 1)] = ("vs", t2)
        b.wait("vector", "vs", t1)
        t3 = b.do("vector", lambda e: e.tensor_mul(
            st[0:1, ST2:ST2 + 512], st[0:1, ST0:ST0 + 512],
            st[0:1, ST0:ST0 + 512]), inc="vs")
        b.wait("vector", "vs", t3)
        t4 = b.do("vector", lambda e: e.tensor_sub(
            st[0:1, ST1:ST1 + 512], st[0:1, ST1:ST1 + 512],
            st[0:1, ST2:ST2 + 512]), inc="vs")
        b.wait("scalar", "vs", t4)
        t5 = b.do("scalar", lambda e: e.activation(
            st[0:1, ST2:ST2 + 512], st[0:1, ST1:ST1 + 512], AF.Sqrt,
            bias=svec[0:1, 178:179]),
            inc="ss")
        b.wait("vector", "ss", t5)

        def _recip_bf(e):
            with nc.allow_low_precision(reason="LN 1/std rounded to bf16"):
                return e.reciprocal(sbf[0:1, 0:512], st[0:1, ST2:ST2 + 512])
        t6 = b.do("vector", _recip_bf, inc="vs")
        b.wait("vector", "vs", t6)
        t7 = b.do("vector", lambda e: e.tensor_mul(
            sbf[0:1, 512:1024], st[0:1, ST0:ST0 + 512],
            sbf[0:1, 0:512]), inc="vs")
        wait_free("tensor", ("pBh", 0))
        wait_free("tensor", ("pBh", 1))
        b.wait("tensor", "vs", t7)
        b.do("tensor", lambda e: e.matmul(
            pB[:, 0:512], ones_bf1[:], sbf[0:1, 0:512],
            start=True, stop=True))
        tok_bc = b.do("tensor", lambda e: e.matmul(
            pB[:, 512:1024], ones_bf1[:], sbf[0:1, 512:1024],
            start=True, stop=True), inc="ts")
        free["st"] = ("ts", tok_bc)
        b.wait("vector", "ts", tok_bc)
        c1 = b.do("vector", lambda e: e.tensor_copy(
            sbf[:, 0:512], pB[:, 0:512]), inc="vs")
        c2 = b.do("vector", lambda e: e.tensor_copy(
            sbf[:, 512:1024], pB[:, 512:1024]), inc="vs")
        free[("pBh", 0)] = ("vs", c2)
        free[("pBh", 1)] = ("vs", c2)
        b.wait("vector", "vs", c2)
        tok_aps = []
        for kc in range(KC):
            ta = b.do("vector", lambda e, kc=kc: e.tensor_mul(
                xn[:, kc * 512:(kc + 1) * 512],
                xn[:, kc * 512:(kc + 1) * 512], sbf[:, 0:512]), inc="vs")
            b.wait("vector", "vs", ta)
            tok_aps.append(b.do("vector", lambda e, kc=kc: e.tensor_sub(
                xn[:, kc * 512:(kc + 1) * 512],
                xn[:, kc * 512:(kc + 1) * 512], sbf[:, 512:1024]), inc="vs"))
        free[("sq", 0)] = ("vs", tok_aps[-1])
        free[("sq", 1)] = ("vs", tok_aps[-1])
        return tok_aps

    def emit_ffn_block(blk, tok_xn):
        tok_xn_list = tok_xn if isinstance(tok_xn, list) else None
        tok_fuse_last = None
        tok_gu_mm_last = None
        for m in range(MCG):
            p = m % 2
            gsl, usl = p * 4096, p * 4096 + 2048
            wait_free("gpsimd", ("gslab", p))
            wait_free("gpsimd", ("uslab", p))
            kg = f"wg{p}"
            tg = b.do("gpsimd", lambda e, m=m, gsl=gsl: e.dma_start(
                out=wsb[:, gsl:gsl + 4096].rearrange("p (g c) -> p g c", g=2),
                in_=wg[m].rearrange("g p c -> p g c")), inc=kg, n=16)
            tu = tg
            ku = kg
            b.wait("tensor", kg, tg)
            if m == 0:
                wait_free("tensor", "fuse")
            wait_free("tensor", ("pAh", p))
            for k in range(KC):
                if m == 0 and tok_xn_list is not None:
                    b.wait("tensor", "vs", tok_xn_list[k])
                elif m == 0 and k == 0:
                    b.wait("tensor", "vs", tok_xn)
                tok = b.do("tensor", lambda e, k=k, p=p, gsl=gsl: e.matmul(
                    pA[:, p * 512:(p + 1) * 512],
                    wsb[:, gsl + k * 128:gsl + (k + 1) * 128],
                    xn[:, k * 512:(k + 1) * 512],
                    start=(k == 0), stop=(k == KC - 1)),
                    inc="ts" if k == KC - 1 else None)
            tok_gmm = tok
            free[("gslab", p)] = ("ts", tok_gmm)
            b.wait("tensor", ku, tu)
            wait_free("tensor", ("pBh", p))
            for k in range(KC):
                tok = b.do("tensor", lambda e, k=k, p=p, usl=usl: e.matmul(
                    pB[:, p * 512:(p + 1) * 512],
                    wsb[:, usl + k * 128:usl + (k + 1) * 128],
                    xn[:, k * 512:(k + 1) * 512],
                    start=(k == 0), stop=(k == KC - 1)),
                    inc="ts" if k == KC - 1 else None)
            tok_umm = tok
            tok_gu_mm_last = tok_umm
            free[("uslab", p)] = ("ts", tok_umm)
            gsf, usf = (p * 2) * 512, (p * 2 + 1) * 512
            b.wait("scalar", "ts", tok_gmm)
            wait_free("scalar", ("gst", p))
            tok_si = b.do("scalar", lambda e, m=m, p=p, gsf=gsf: e.activation(
                gu[:, gsf:gsf + 512], pA[:, p * 512:(p + 1) * 512], AF.Silu,
                scale=svec[:, SG + m:SG + m + 1]), inc="ss")
            free[("pAh", p)] = ("ss", tok_si)
            b.wait("vector", "ts", tok_umm)
            wait_free("vector", ("gst", p))
            tok_ue = b.do("vector", lambda e, m=m, p=p, usf=usf: e.tensor_scalar_mul(
                gu[:, usf:usf + 512], pB[:, p * 512:(p + 1) * 512],
                svec[:, SU + m:SU + m + 1]), inc="vs")
            free[("pBh", p)] = ("vs", tok_ue)
            b.wait("vector", "ss", tok_si)
            b.wait("vector", "vs", tok_ue)
            tok_fuse_last = b.do("vector", lambda e, m=m, gsf=gsf, usf=usf: e.tensor_mul(
                big[:, m * 512:(m + 1) * 512], gu[:, gsf:gsf + 512],
                gu[:, usf:usf + 512]), inc="vs")
            free[("gst", p)] = ("vs", tok_fuse_last)
        tok_mm_d = None
        tok_devac = None
        for mo in range(16):
            q = mo % 2
            for sub in range(4):
                ss2 = (mo * 4 + sub) % 2
                dsl = 8192 + ss2 * 2048
                kd = f"wd{ss2}"
                wait_free("gpsimd", ("dslab", ss2))
                td = b.do("gpsimd", lambda e, mo=mo, sub=sub, dsl=dsl: e.dma_start(
                    out=wsb[:, dsl:dsl + 2048], in_=wd[mo, sub]), inc=kd, n=16)
                b.wait("tensor", kd, td)
                if mo == 0 and sub == 0:
                    b.wait("tensor", "vs", tok_fuse_last)
                if sub == 0:
                    wait_free("tensor", ("pCh", q))
                for kk in range(16):
                    k = sub * 16 + kk
                    tok_mm_d = b.do("tensor", lambda e, k=k, kk=kk, q=q, dsl=dsl: e.matmul(
                        pC[:, q * 512:(q + 1) * 512],
                        wsb[:, dsl + kk * 128:dsl + (kk + 1) * 128],
                        big[:, k * 512:(k + 1) * 512],
                        start=(k == 0), stop=(k == MCG - 1)),
                        inc="ts" if kk == 15 else None)
                free[("dslab", ss2)] = ("ts", tok_mm_d)
            b.wait("vector", "ts", tok_mm_d)
            xcol = mo * 1024 + blk * 512
            tok_devac = b.do("vector", lambda e, q=q, xcol=xcol, mo=mo: e.scalar_tensor_tensor(
                xT[:, xcol:xcol + 512], pC[:, q * 512:(q + 1) * 512],
                svec[:, SD + mo:SD + mo + 1], xT[:, xcol:xcol + 512],
                ALU.mult, ALU.add), inc="vs")
            free[("pCh", q)] = ("vs", tok_devac)
        free["fuse"] = ("ts", tok_mm_d)
        return tok_devac, tok_mm_d, tok_gu_mm_last

    tok_xn_free = None
    tok_ffn_evac = tok_ffn_mm = None
    for blk in (0, 1):
        tok_xn = emit_ln(blk, tok_xn_free)
        tok_ffn_evac, tok_ffn_mm, tok_xn_free = emit_ffn_block(blk, tok_xn)
    tok_ffn1_mm = tok_ffn_mm

    def emit_store(waits):
        for s, t in waits:
            b.wait("sync", s, t)
        b.do("sync", lambda e: e.dma_start(
            out=yTd.rearrange("(kc p) t -> p kc t", p=128),
            in_=xT[:, :].rearrange("p (kc t) -> p kc t", kc=KC)), inc="sto", n=16)
        b.wait("sync", "sto", 16)

    if STAGE == "ffn1":
        emit_store([("vs", tok_ffn_evac)])
        b.emit_block()
        es.close()
        return nc

    tok_qev = {}
    tok_qkv_mm = None
    qi = 0

    def emit_qkv_m(blk, m, tok_xn2, chase=False):
        nonlocal tok_qkv_mm, qi
        p = qi % 2
        qi += 1
        qsl = p * 2048
        kq = f"wq{p}"
        if qi <= 2:
            b.wait("gpsimd", "ts", tok_ffn1_mm)
        wait_free("gpsimd", ("qslab", p))
        tq = b.do("gpsimd", lambda e, m=m, qsl=qsl: e.dma_start(
            out=wsb[:, qsl:qsl + 2048], in_=wq[m]), inc=kq, n=16)
        b.wait("tensor", kq, tq)
        if not chase:
            b.wait("tensor", "vs",
                   tok_xn2[-1] if isinstance(tok_xn2, list) else tok_xn2)
        wait_free("tensor", ("pCh", p))
        for k in range(KC):
            if chase and isinstance(tok_xn2, list):
                b.wait("tensor", "vs", tok_xn2[k])
            tok_qkv_mm = b.do("tensor", lambda e, k=k, p=p, qsl=qsl: e.matmul(
                pC[:, p * 512:(p + 1) * 512],
                wsb[:, qsl + k * 128:qsl + (k + 1) * 128],
                xn[:, k * 512:(k + 1) * 512],
                start=(k == 0), stop=(k == KC - 1)),
                inc="ts" if k == KC - 1 else None)
        free[("qslab", p)] = ("ts", tok_qkv_mm)
        if m < 16:
            darr, dcol = big, m * 1024 + blk * 512
        elif m == 16:
            darr, dcol = sbf, 1024 + blk * 512
        else:
            darr, dcol = sbf, 3072 + blk * 512
        b.wait("scalar", "ts", tok_qkv_mm)
        tok_ev = b.do("scalar", lambda e, m=m, p=p, darr=darr, dcol=dcol: e.activation(
            darr[:, dcol:dcol + 512], pC[:, p * 512:(p + 1) * 512],
            AF.Identity, bias=svec[:, BQ + m:BQ + m + 1]), inc="ss")
        free[("pCh", p)] = ("ss", tok_ev)
        tok_qev[(blk, m)] = tok_ev

    tok_xn2_blk = {}
    tok_xn2_blk[0] = emit_ln(0, tok_xn_free)
    first = True
    for m in (16, 17) + tuple(range(16)):
        emit_qkv_m(0, m, tok_xn2_blk[0], chase=first)
        first = False
    tok_xn_free = tok_qkv_mm
    tok_xn2_blk[1] = emit_ln(1, tok_xn_free)
    first = True
    for m in (16, 17):
        emit_qkv_m(1, m, tok_xn2_blk[1], chase=first)
        first = False

    b.wait("tensor", "gs", tok_ident)
    b.wait("tensor", "ss", tok_qev[(0, 17)])
    b.wait("tensor", "ss", tok_qev[(1, 17)])
    tok_tp = None
    for j in range(8):
        tok_tp = b.do("tensor", lambda e, j=j: e.transpose(
            pT[:, j * 128:(j + 1) * 128],
            sbf[:, 3072 + j * 128:3072 + (j + 1) * 128], ident[:]), inc="ts")
    b.wait("vector", "ts", tok_tp)
    tok_vtc = b.do("vector", lambda e: e.tensor_copy(
        sbf[:, 2048:3072], pT[:, 0:1024]), inc="vs")
    b.wait("sync", "ss", tok_qev[(1, 16)])
    b.wait("sync", "ss", tok_qev[(0, 16)])
    b.wait("sync", "vs", tok_vtc)
    b.do("sync", lambda e: e.dma_start(out=ccin.ap(), in_=sbf[:, 1024:3072]),
         inc="ccd", n=16)

    for m in range(2):
        emit_qkv_m(1, m, tok_xn2_blk[1])

    b.wait("gpsimd", "ccd", 16)
    b.do("gpsimd", lambda e: e.collective_compute(
        "AllGather", ALU.bypass, replica_groups=[list(range(NC))],
        ins=[ccin.ap().opt()], outs=[ccout.ap().opt()]), inc="ccs")
    b.wait("gpsimd", "ts", tok_ffn1_mm)
    tok_vmset = b.do("gpsimd", lambda e: e.memset(big[:, VOFF:BIGC], 1.0),
                     inc="gs")

    for m in range(2, 16):
        emit_qkv_m(1, m, tok_xn2_blk[1])
    tok_xn_free = tok_qkv_mm

    b.wait("gpsimd", "ts", tok_qkv_mm)
    tok_masks = None
    for d in range(4):
        tok_ms = b.do("gpsimd", lambda e, d=d: e.memset(
            xn[:, d * 512:(d + 1) * 512], 1.0), inc="gs")
        b.wait("gpsimd", "gs", tok_ms)
        tok_masks = b.do("gpsimd", lambda e, d=d: e.affine_select(
            out=xn[:, d * 512:(d + 1) * 512], in_=xn[:, d * 512:(d + 1) * 512],
            compare_op=ALU.is_ge, fill=0.0, base=-(128 * d),
            pattern=[[1, 512]], channel_multiplier=-1), inc="gs")

    b.wait("sync", "ccs", 1)
    b.wait("sync", "ts", tok_ffn1_mm)
    b.wait("sync", "gs", tok_vmset)
    co = ccout.ap()
    for r in range(NC):
        for jb, gb in ((0, r), (1, 15 - r)):
            b.do("sync", lambda e, r=r, jb=jb, gb=gb: e.dma_start(
                out=big[:, KOFF + gb * 512:KOFF + gb * 512 + 512],
                in_=co[r * 128:(r + 1) * 128, jb * 512:(jb + 1) * 512]),
                inc="krb", n=16)
    for r in range(NC):
        for jb, gb in ((0, r), (1, 15 - r)):
            for hf in range(2):
                b.do("sync", lambda e, r=r, jb=jb, gb=gb, hf=hf: e.dma_start(
                    out=big[:, VOFF + 4 * gb * 130:VOFF + 4 * gb * 130 + 520]
                        .rearrange("p (j r) -> p j r", r=130)
                        [:, :, hf * 65:hf * 65 + 64],
                    in_=co[r * 128:(r + 1) * 128,
                           1024 + jb * 512:1024 + (jb + 1) * 512]
                        .rearrange("p (j r) -> p j r", r=128)
                        [:, :, hf * 64:hf * 64 + 64]),
                    inc="krv", n=16)
    pre_tot = {s: b.cnt[s] for s in ("ts", "vs", "ss", "gs", "krb", "ccd")}
    krv_total = b.cnt["krv"]

    b.emit_block()

    sems = b.sems
    engines = [nc.tensor, nc.scalar, nc.vector, nc.gpsimd]
    idx = [nc.partition_id(engines=[ET.PE]),
           nc.partition_id(engines=[ET.Activation]),
           nc.partition_id(engines=[ET.DVE]),
           nc.partition_id(engines=[ET.Pool])]

    for body in nc.Switch(engines=engines, index=idx, n=NC):
        emit_attention_body(nc, sems, body, pre_tot, krv_total, xT, xn,
                            big, wsb, gu, sbf, st, svec, ones_bf1,
                            pA, pB, pC, pD, pT, wp)

    if STAGE == "attn":
        b.wait("sync", "at_done", 4)
        emit_store([])
        b.emit_block()
        es.close()
        return nc

    for key in [("pAh", 0), ("pAh", 1), ("pBh", 0), ("pBh", 1),
                ("pCh", 0), ("pCh", 1), "fuse", "st",
                ("gslab", 0), ("gslab", 1), ("uslab", 0), ("uslab", 1),
                ("dslab", 0), ("dslab", 1), ("sq", 0), ("sq", 1),
                ("gst", 0), ("gst", 1)]:
        free[key] = None
    tok_xn_free = tok_qkv_mm
    for blk in (0, 1):
        tok_xn = emit_ln(blk, tok_xn_free)
        tok_ffn_evac, tok_ffn_mm, tok_xn_free = emit_ffn_block(blk, tok_xn)

    emit_store([("vs", tok_ffn_evac)])
    b.emit_block()
    es.close()
    return nc


def emit_attention_body(nc, sems, c, pre_tot, krv_total, xT, xn,
                        big, wsb, gu, sbf, st, svec, ones_bf1,
                        pA, pB, pC, pD, pT, wp):
    at = {k: 0 for k in ("t", "v", "s", "g", "d0", "d1")}
    S_ = {"t": sems["at_t"], "v": sems["at_v"], "s": sems["at_s"],
          "g": sems["at_g"], "d0": sems["at_d0"], "d1": sems["at_d1"]}
    E = {"t": nc.tensor, "v": nc.vector, "s": nc.scalar, "g": nc.gpsimd}

    def inc(key, ins, n=1):
        at[key] += n
        ins.then_inc(S_[key], n)
        return at[key]

    def w(ekey, skey, tok):
        if tok and tok > 0:
            E[ekey].wait_ge(S_[skey], tok)

    for ekey in ("t", "s", "v", "g"):
        for s, v in pre_tot.items():
            if v > 0:
                E[ekey].wait_ge(sems[s], v)
    st_body = {"krv_waited": False}

    fre = {}

    def wfree(ekey, key):
        ent = fre.get(key)
        if ent is not None:
            w(ekey, ent[0], ent[1])

    pending = []
    for qbl, gb in ((0, c), (1, 15 - c)):
        nk = 4 * (gb + 1)
        for h in range(NH):
            qh = big[:, h * 1024 + qbl * 512: h * 1024 + qbl * 512 + 512]
            pvb = pB if h % 2 == 0 else pC
            pvname = ("pv", h % 2)

            def sc_ap(kt):
                r = kt % 3
                return pD[:, 0:512] if r == 2 else pA[:, r * 512:(r + 1) * 512]

            def scores(kt):
                wfree("t", ("scr", kt % 3))
                ins = nc.tensor.matmul(
                    sc_ap(kt), big[:, KOFF + kt * 128:KOFF + (kt + 1) * 128],
                    qh, start=True, stop=True)
                return inc("t", ins)

            tok_sc = {0: scores(0), 1: scores(1)}
            while pending:
                pending.pop(0)()
            tok_exp = {}
            tok_mask = {}
            tok_pv = None

            def emit_pv(kt):
                nonlocal tok_pv
                sl = kt % 4
                if not st_body["krv_waited"]:
                    nc.tensor.wait_ge(sems["krv"], krv_total)
                    st_body["krv_waited"] = True
                w("t", "s", tok_exp[kt])
                if kt in tok_mask:
                    w("t", "v", tok_mask[kt])
                if kt == 0:
                    wfree("t", pvname)
                va = VOFF + kt * 130
                nc.tensor.matmul(pvb[0:65, 0:512], big[:, va:va + 65],
                                 sbf[:, sl * 512:(sl + 1) * 512],
                                 start=(kt == 0), stop=(kt == nk - 1))
                ins = nc.tensor.matmul(pvb[0:65, 512:1024],
                                       big[:, va + 65:va + 130],
                                       sbf[:, sl * 512:(sl + 1) * 512],
                                       start=(kt == 0), stop=(kt == nk - 1))
                tok_pv = inc("t", ins)
                fre[("probs", sl)] = ("t", tok_pv)

            for kt in range(nk):
                sl = kt % 4
                w("s", "t", tok_sc[kt])
                wfree("s", ("probs", sl))
                ins = nc.scalar.activation(
                    sbf[:, sl * 512:(sl + 1) * 512],
                    sc_ap(kt), AF.Exp, scale=SCALE)
                tok_exp[kt] = inc("s", ins)
                fre[("scr", kt % 3)] = ("s", tok_exp[kt])
                if kt >= nk - 4:
                    d = kt - (nk - 4)
                    w("v", "s", tok_exp[kt])
                    ins = nc.vector.tensor_mul(
                        sbf[:, sl * 512:(sl + 1) * 512],
                        sbf[:, sl * 512:(sl + 1) * 512],
                        xn[:, d * 512:(d + 1) * 512])
                    tok_mask[kt] = inc("v", ins)
                if kt + 2 < nk:
                    tok_sc[kt + 2] = scores(kt + 2)
                if kt >= 1:
                    emit_pv(kt - 1)
            emit_pv(nk - 1)
            w("v", "t", tok_pv)
            wfree("v", ("strow", h % 2))
            rrow = sbf[0:1, 2048 + (h % 2) * 512:2048 + (h % 2) * 512 + 512]
            with nc.allow_low_precision(reason="softmax 1/sum rounded to bf16"):
                ins = nc.vector.reciprocal(rrow, pvb[64:65, 0:512])
            tok_rec = inc("v", ins)

            def flush_norm(h=h, pvb=pvb, pvname=pvname, tok_rec=tok_rec,
                           rrow=rrow):
                w("t", "v", tok_rec)
                wfree("t", ("scr", 2))
                ins = nc.tensor.matmul(pD[:, 0:512], ones_bf1[:], rrow,
                                       start=True, stop=True)
                tok_bc = inc("t", ins)
                fre[("strow", h % 2)] = ("t", tok_bc)
                w("v", "t", tok_bc)
                ins = nc.vector.tensor_copy(gu[:, 0:512], pD[:, 0:512])
                tok_rb = inc("v", ins)
                fre[("scr", 2)] = ("v", tok_rb)
                oc = OSB + h * 512
                w("v", "v", tok_rb)
                ins = nc.vector.tensor_mul(
                    wsb[0:64, oc:oc + 512], pvb[0:64, 0:512], gu[0:64, 0:512])
                inc("v", ins)
                ins = nc.vector.tensor_mul(
                    wsb[64:128, oc:oc + 512], pvb[0:64, 512:1024],
                    gu[64:128, 0:512])
                tok_norm = inc("v", ins)
                fre[pvname] = ("v", tok_norm)
            pending.append(flush_norm)
        while pending:
            pending.pop(0)()
        for m in range(16):
            p = m % 2
            dkey = f"d{p}"
            wfree("g", ("pslab", p))
            ins = nc.gpsimd.dma_start(out=wsb[:, p * 2048:(p + 1) * 2048],
                                      in_=wp[m])
            tok_w = inc(dkey, ins, 16)
            w("t", dkey, tok_w)
            if m == 0:
                w("t", "s", at["s"])
                w("t", "v", at["v"])
            wfree("t", ("pAp", p))
            for k in range(KC):
                ins = nc.tensor.matmul(
                    pA[:, p * 512:(p + 1) * 512],
                    wsb[:, p * 2048 + k * 128:p * 2048 + (k + 1) * 128],
                    wsb[:, OSB + k * 512:OSB + (k + 1) * 512],
                    start=(k == 0), stop=(k == KC - 1))
            tok_pm = inc("t", ins)
            fre[("pslab", p)] = ("t", tok_pm)
            w("v", "t", tok_pm)
            xcol = m * 1024 + qbl * 512
            ins = nc.vector.scalar_tensor_tensor(
                xT[:, xcol:xcol + 512],
                pA[:, p * 512:(p + 1) * 512],
                svec[:, BP + m:BP + m + 1],
                xT[:, xcol:xcol + 512],
                ALU.add, ALU.add)
            tok_pe = inc("v", ins)
            fre[("pAp", p)] = ("v", tok_pe)
        fre[("scr", 0)] = ("v", at["v"])
        fre[("scr", 1)] = ("v", at["v"])

    nc.tensor.wait_ge(S_["t"], at["t"])
    nc.scalar.wait_ge(S_["s"], at["s"])
    nc.vector.wait_ge(S_["v"], at["v"])
    nc.gpsimd.wait_ge(S_["g"], at["g"])
    for eng in (nc.tensor, nc.scalar, nc.vector, nc.gpsimd):
        eng.sem_inc(sems["at_done"], 1)
    for eng in (nc.tensor, nc.scalar, nc.vector, nc.gpsimd):
        eng.wait_ge(sems["at_done"], 4)


def _shard(inputs):
    x = np.asarray(inputs["L0_attn_x"], dtype=np.float32).reshape(8192, H)
    xT = np.ascontiguousarray(x.T)
    wgate = np.asarray(inputs["L0_ffn_w_gate"])
    wup = np.asarray(inputs["L0_ffn_w_up"])
    wdown = np.asarray(inputs["L0_ffn_w_down"])
    Wq = np.asarray(inputs["L0_attn_W_qkv"], dtype=np.float32)
    Wp = np.asarray(inputs["L0_attn_W_proj"], dtype=np.float32)

    wg_only = wgate.reshape(16, 128, 64, 128).transpose(2, 1, 0, 3).reshape(64, 1, 128, 2048)
    wu_only = wup.reshape(16, 128, 64, 128).transpose(2, 1, 0, 3).reshape(64, 1, 128, 2048)
    wg_t = np.ascontiguousarray(np.concatenate([wg_only, wu_only], axis=1))
    wd_t = np.ascontiguousarray(
        wdown.reshape(64, 128, 16, 128).transpose(2, 1, 0, 3)
        .reshape(16, 128, 4, 2048).transpose(0, 2, 1, 3))
    wq_t = np.ascontiguousarray(
        Wq.reshape(16, 128, 18, 128).transpose(2, 1, 0, 3).reshape(18, 128, 2048))
    wp_t = np.ascontiguousarray(
        Wp.reshape(16, 128, 16, 128).transpose(2, 1, 0, 3).reshape(16, 128, 2048))

    vec = np.zeros((128, 180), dtype=np.float32)
    vec[:, SG:SG + 64] = np.asarray(inputs["L0_ffn_s_gate"],
                                    dtype=np.float32).reshape(64, 128).T
    vec[:, SU:SU + 64] = np.asarray(inputs["L0_ffn_s_up"],
                                    dtype=np.float32).reshape(64, 128).T
    vec[:, SD:SD + 16] = np.asarray(inputs["L0_ffn_s_down"],
                                    dtype=np.float32).reshape(16, 128).T
    vec[:, BQ:BQ + 18] = np.asarray(inputs["L0_attn_b_qkv"],
                                    dtype=np.float32).reshape(18, 128).T
    vec[:, BP:BP + 16] = np.asarray(inputs["L0_attn_b_proj"],
                                    dtype=np.float32).reshape(16, 128).T
    vec[:, 178] = EPS

    in_maps = []
    for c in range(NC):
        xTc = np.ascontiguousarray(np.concatenate(
            [xT[:, 512 * c:512 * (c + 1)],
             xT[:, 512 * (15 - c):512 * (16 - c)]], axis=1))
        in_maps.append(dict(xT=xTc, wg=wg_t, wd=wd_t, wq=wq_t,
                            wp=wp_t, vecs=vec))
    return in_maps


_NC_CACHE = {}


def _tune_compiler_flags():
    try:
        from concourse.compiler_utils import get_compiler_flags, set_compiler_flags
        flags = get_compiler_flags()
        out = []
        for f in flags:
            if f.startswith("--internal-backend-options="):
                f = f.replace("--enable-ldw-opt=false", "--enable-ldw-opt=true")
            out.append(f)
        set_compiler_flags(out)
    except Exception:
        pass


def kernel(**inputs):
    _tune_compiler_flags()
    in_maps = _shard(inputs)
    if STAGE not in _NC_CACHE:
        _NC_CACHE[STAGE] = build_nc()
    nc = _NC_CACHE[STAGE]
    res = run_bass_kernel_spmd(nc, in_maps, core_ids=list(range(NC)))
    yT = np.empty((H, 8192), dtype=np.float32)
    for c in range(NC):
        yc = res.results[c]["yT"]
        yT[:, 512 * c:512 * (c + 1)] = yc[:, 0:512]
        yT[:, 512 * (15 - c):512 * (16 - c)] = yc[:, 512:1024]
    return np.ascontiguousarray(yT.T).reshape(1, 8192, H)


# version 14
# speedup vs baseline: 1.0242x; 1.0178x over previous
import contextlib
import numpy as np
import ml_dtypes
import concourse.bass as bass
import concourse.mybir as mybir
from concourse.bass_utils import run_bass_kernel_spmd

F32 = mybir.dt.float32
BF16 = mybir.dt.bfloat16
I8 = mybir.dt.int8
AF = mybir.ActivationFunctionType
ALU = mybir.AluOpType
ET = mybir.EngineType

H = 2048
NH = 16
INTER = 8192
NC = 8
KC = 16
MCG = 64
SCALE = 128.0 ** -0.5
EPS = 1e-5

QOFF = 0
KOFF = 16384
VOFF = 24576
BIGC = VOFF + 64 * 130

OSB = 4096

SG, SU, SD, BQ, BP = 0, 64, 128, 144, 162
ST0, ST1, ST2 = 0, 512, 1024

STAGE = "full"


class Prog:
    ENGS = ("tensor", "vector", "scalar", "gpsimd", "sync")

    def __init__(self, nc, es):
        self.nc = nc
        self.es = es
        self.q = {e: [] for e in self.ENGS}
        self.sems = {}
        self.cnt = {}

    def sem(self, name):
        self.sems[name] = self.es.enter_context(self.nc.semaphore(name))
        self.cnt[name] = 0

    def do(self, eng, fn, inc=None, n=1):
        tok = None
        if inc is not None:
            self.cnt[inc] += n
            tok = self.cnt[inc]
            sem = self.sems[inc]
            self.q[eng].append(lambda e, fn=fn, sem=sem, n=n: fn(e).then_inc(sem, n))
        else:
            self.q[eng].append(lambda e, fn=fn: fn(e))
        return tok

    def wait(self, eng, semname, tok):
        if tok is None or tok <= 0:
            return
        sem = self.sems[semname]
        self.q[eng].append(lambda e, sem=sem, tok=tok: e.wait_ge(sem, tok))

    def emit_block(self):
        nonempty = [e for e in self.ENGS if self.q[e]]
        if not nonempty:
            return
        with self.nc.Block() as block:
            for engname in nonempty:
                lst = self.q[engname]
                getattr(block, engname)(lambda e, lst=lst: [f(e) for f in lst])
        self.q = {e: [] for e in self.ENGS}


def build_nc():
    nc = bass.Bass("TRN2", target_bir_lowering=False)

    xTd = nc.dram_tensor("xT", [H, 1024], F32, kind="ExternalInput").ap()
    wg = nc.dram_tensor("wg", [MCG, 2, 128, 2048], I8, kind="ExternalInput").ap()
    wd = nc.dram_tensor("wd", [16, 4, 128, 2048], I8, kind="ExternalInput").ap()
    wq = nc.dram_tensor("wq", [18, 128, 2048], F32, kind="ExternalInput").ap()
    wp = nc.dram_tensor("wp", [16, 128, 2048], F32, kind="ExternalInput").ap()
    vecs = nc.dram_tensor("vecs", [128, 180], F32, kind="ExternalInput").ap()
    yTd = nc.dram_tensor("yT", [H, 1024], F32, kind="ExternalOutput").ap()
    ccin = nc.dram_tensor("ccin", [128, 2048], BF16, kind="Internal")
    ccout = nc.dram_tensor("ccout", [1024, 2048], BF16, kind="Internal", addr_space="Shared")

    es = contextlib.ExitStack()
    sb = lambda n, s, d: es.enter_context(nc.sbuf_tensor(n, s, d))
    ps = lambda n, s, d: es.enter_context(nc.psum_tensor(n, s, d))

    xT = sb("xTs", [128, KC * 1024], F32)
    xn = sb("xns", [128, KC * 512], BF16)
    big = sb("bigs", [128, BIGC], BF16)
    wsb = sb("wsbs", [128, 12288], BF16)
    gu = sb("gus", [128, 2048], F32)
    sbf = sb("sbfs", [128, 4096], BF16)
    svec = sb("svecs", [128, 180], F32)
    st = sb("sts", [1, 1536], F32)
    ones_bf = sb("onesb", [128, 1], BF16)
    ones_bf1 = sb("onesb1", [1, 128], BF16)
    ones_f1 = sb("onesf", [1, 128], F32)
    ident = sb("idents", [128, 128], BF16)

    pA = ps("pA", [128, 1024], F32)
    pB = ps("pB", [128, 1024], F32)
    pC = ps("pC", [128, 1024], F32)
    pD = ps("pD", [128, 512], F32)
    pT = ps("pT", [128, 1024], BF16)

    b = Prog(nc, es)
    for s in ("ts", "vs", "ss", "gs", "xl", "vl", "ccd", "ccs", "krb", "krv", "sto",
              "wg0", "wg1", "wu0", "wu1", "wd0", "wd1", "wq0", "wq1", "wq2", "wq3"):
        b.sem(s)
    for s in ("at_t", "at_v", "at_s", "at_g", "at_d0", "at_d1", "at_d2", "at_d3", "at_done"):
        b.sem(s)

    free = {}

    def wait_free(eng, key):
        ent = free.get(key)
        if ent is not None:
            b.wait(eng, ent[0], ent[1])

    b.do("sync", lambda e: e.dma_start(
        out=xT[:, :].rearrange("p (kc t) -> p kc t", kc=KC),
        in_=xTd.rearrange("(kc p) t -> p kc t", p=128)), inc="xl", n=16)
    b.do("sync", lambda e: e.dma_start(out=svec[:, :], in_=vecs), inc="vl", n=16)

    b.do("gpsimd", lambda e: e.memset(ones_bf[:], 1.0), inc="gs")
    b.do("gpsimd", lambda e: e.memset(ones_bf1[:], 1.0), inc="gs")
    tok_ones = b.do("gpsimd", lambda e: e.memset(ones_f1[:], 1.0), inc="gs")
    tok_identm = b.do("gpsimd", lambda e: e.memset(ident[:], 0.0), inc="gs")
    b.wait("gpsimd", "gs", tok_identm)
    tok_ident = b.do("gpsimd", lambda e: e.affine_select(
        out=ident[:], in_=ident[:], compare_op=ALU.not_equal, fill=1.0,
        base=0, pattern=[[-1, 128]], channel_multiplier=1), inc="gs")

    b.wait("vector", "vl", 16)
    b.do("vector", lambda e: e.tensor_scalar_mul(
        svec[:, SD:SD + 16], svec[:, SD:SD + 16], 0.5), inc="vs")
    b.wait("scalar", "vl", 16)
    b.wait("vector", "xl", 16)
    b.wait("scalar", "xl", 16)

    def emit_ln(blk, tok_xn_free):
        xc = lambda kc: kc * 1024 + blk * 512
        b.wait("tensor", "gs", 3)
        b.wait("vector", "ts", tok_xn_free)
        wait_free("vector", "st")
        wait_free("tensor", ("pAh", 0))
        wait_free("tensor", ("pAh", 1))
        tok_m = None
        for kc in range(KC):
            tok_c = b.do("vector", lambda e, kc=kc: e.tensor_copy(
                xn[:, kc * 512:(kc + 1) * 512], xT[:, xc(kc):xc(kc) + 512]),
                inc="vs")
            sl = (kc % 2) * 512
            if kc % 2 == 0:
                wait_free("scalar", ("sq", 0))
                tok_sq = b.do("scalar", lambda e, kc=kc, sl=sl: e.activation(
                    sbf[:, sl:sl + 512], xT[:, xc(kc):xc(kc) + 512], AF.Square),
                    inc="ss")
                sqsem = "ss"
            else:
                wait_free("vector", ("sq", 1))
                tok_sq = b.do("vector", lambda e, kc=kc, sl=sl: e.tensor_mul(
                    sbf[:, sl:sl + 512], xT[:, xc(kc):xc(kc) + 512],
                    xT[:, xc(kc):xc(kc) + 512]), inc="vs")
                sqsem = "vs"
            b.wait("tensor", "vs", tok_c)
            b.do("tensor", lambda e, kc=kc: e.matmul(
                pA[0:1, 0:512], ones_bf[:], xn[:, kc * 512:(kc + 1) * 512],
                start=(kc == 0), stop=(kc == KC - 1)))
            b.wait("tensor", sqsem, tok_sq)
            tok_m = b.do("tensor", lambda e, kc=kc, sl=sl: e.matmul(
                pA[0:1, 512:1024], ones_bf[:], sbf[:, sl:sl + 512],
                start=(kc == 0), stop=(kc == KC - 1)), inc="ts")
            free[("sq", kc % 2)] = ("ts", tok_m)
        b.wait("vector", "ts", tok_m)
        t1 = b.do("vector", lambda e: e.tensor_scalar_mul(
            st[0:1, ST0:ST0 + 512], pA[0:1, 0:512], 1.0 / H), inc="vs")
        t2 = b.do("vector", lambda e: e.tensor_scalar_mul(
            st[0:1, ST1:ST1 + 512], pA[0:1, 512:1024], 1.0 / H), inc="vs")
        free[("pAh", 0)] = ("vs", t1)
        free[("pAh", 1)] = ("vs", t2)
        b.wait("vector", "vs", t1)
        t3 = b.do("vector", lambda e: e.tensor_mul(
            st[0:1, ST2:ST2 + 512], st[0:1, ST0:ST0 + 512],
            st[0:1, ST0:ST0 + 512]), inc="vs")
        b.wait("vector", "vs", t3)
        t4 = b.do("vector", lambda e: e.tensor_sub(
            st[0:1, ST1:ST1 + 512], st[0:1, ST1:ST1 + 512],
            st[0:1, ST2:ST2 + 512]), inc="vs")
        b.wait("scalar", "vs", t4)
        t5 = b.do("scalar", lambda e: e.activation(
            st[0:1, ST2:ST2 + 512], st[0:1, ST1:ST1 + 512], AF.Sqrt,
            bias=svec[0:1, 178:179]),
            inc="ss")
        b.wait("vector", "ss", t5)

        def _recip_bf(e):
            with nc.allow_low_precision(reason="LN 1/std rounded to bf16"):
                return e.reciprocal(sbf[0:1, 0:512], st[0:1, ST2:ST2 + 512])
        t6 = b.do("vector", _recip_bf, inc="vs")
        b.wait("vector", "vs", t6)
        t7 = b.do("vector", lambda e: e.tensor_mul(
            sbf[0:1, 512:1024], st[0:1, ST0:ST0 + 512],
            sbf[0:1, 0:512]), inc="vs")
        wait_free("tensor", ("pBh", 0))
        wait_free("tensor", ("pBh", 1))
        b.wait("tensor", "vs", t7)
        b.do("tensor", lambda e: e.matmul(
            pB[:, 0:512], ones_bf1[:], sbf[0:1, 0:512],
            start=True, stop=True))
        tok_bc = b.do("tensor", lambda e: e.matmul(
            pB[:, 512:1024], ones_bf1[:], sbf[0:1, 512:1024],
            start=True, stop=True), inc="ts")
        free["st"] = ("ts", tok_bc)
        b.wait("vector", "ts", tok_bc)
        c1 = b.do("vector", lambda e: e.tensor_copy(
            sbf[:, 0:512], pB[:, 0:512]), inc="vs")
        c2 = b.do("vector", lambda e: e.tensor_copy(
            sbf[:, 512:1024], pB[:, 512:1024]), inc="vs")
        free[("pBh", 0)] = ("vs", c2)
        free[("pBh", 1)] = ("vs", c2)
        b.wait("vector", "vs", c2)
        tok_aps = []
        for kc in range(KC):
            ta = b.do("vector", lambda e, kc=kc: e.tensor_mul(
                xn[:, kc * 512:(kc + 1) * 512],
                xn[:, kc * 512:(kc + 1) * 512], sbf[:, 0:512]), inc="vs")
            b.wait("vector", "vs", ta)
            tok_aps.append(b.do("vector", lambda e, kc=kc: e.tensor_sub(
                xn[:, kc * 512:(kc + 1) * 512],
                xn[:, kc * 512:(kc + 1) * 512], sbf[:, 512:1024]), inc="vs"))
        free[("sq", 0)] = ("vs", tok_aps[-1])
        free[("sq", 1)] = ("vs", tok_aps[-1])
        return tok_aps

    def emit_ffn_block(blk, tok_xn):
        tok_xn_list = tok_xn if isinstance(tok_xn, list) else None
        tok_fuse_last = None
        tok_gu_mm_last = None
        for m in range(MCG):
            p = m % 2
            gsl, usl = p * 4096, p * 4096 + 2048
            wait_free("gpsimd", ("gslab", p))
            wait_free("gpsimd", ("uslab", p))
            kg = f"wg{p}"
            tg = b.do("gpsimd", lambda e, m=m, gsl=gsl: e.dma_start(
                out=wsb[:, gsl:gsl + 4096].rearrange("p (g c) -> p g c", g=2),
                in_=wg[m].rearrange("g p c -> p g c")), inc=kg, n=16)
            tu = tg
            ku = kg
            b.wait("tensor", kg, tg)
            if m == 0:
                wait_free("tensor", "fuse")
            wait_free("tensor", ("pAh", p))
            for k in range(KC):
                if m == 0 and tok_xn_list is not None:
                    b.wait("tensor", "vs", tok_xn_list[k])
                elif m == 0 and k == 0:
                    b.wait("tensor", "vs", tok_xn)
                tok = b.do("tensor", lambda e, k=k, p=p, gsl=gsl: e.matmul(
                    pA[:, p * 512:(p + 1) * 512],
                    wsb[:, gsl + k * 128:gsl + (k + 1) * 128],
                    xn[:, k * 512:(k + 1) * 512],
                    start=(k == 0), stop=(k == KC - 1)),
                    inc="ts" if k == KC - 1 else None)
            tok_gmm = tok
            free[("gslab", p)] = ("ts", tok_gmm)
            b.wait("tensor", ku, tu)
            wait_free("tensor", ("pBh", p))
            for k in range(KC):
                tok = b.do("tensor", lambda e, k=k, p=p, usl=usl: e.matmul(
                    pB[:, p * 512:(p + 1) * 512],
                    wsb[:, usl + k * 128:usl + (k + 1) * 128],
                    xn[:, k * 512:(k + 1) * 512],
                    start=(k == 0), stop=(k == KC - 1)),
                    inc="ts" if k == KC - 1 else None)
            tok_umm = tok
            tok_gu_mm_last = tok_umm
            free[("uslab", p)] = ("ts", tok_umm)
            gsf, usf = (p * 2) * 512, (p * 2 + 1) * 512
            b.wait("scalar", "ts", tok_gmm)
            wait_free("scalar", ("gst", p))
            tok_si = b.do("scalar", lambda e, m=m, p=p, gsf=gsf: e.activation(
                gu[:, gsf:gsf + 512], pA[:, p * 512:(p + 1) * 512], AF.Silu,
                scale=svec[:, SG + m:SG + m + 1]), inc="ss")
            free[("pAh", p)] = ("ss", tok_si)
            b.wait("vector", "ts", tok_umm)
            wait_free("vector", ("gst", p))
            tok_ue = b.do("vector", lambda e, m=m, p=p, usf=usf: e.tensor_scalar_mul(
                gu[:, usf:usf + 512], pB[:, p * 512:(p + 1) * 512],
                svec[:, SU + m:SU + m + 1]), inc="vs")
            free[("pBh", p)] = ("vs", tok_ue)
            b.wait("vector", "ss", tok_si)
            b.wait("vector", "vs", tok_ue)
            tok_fuse_last = b.do("vector", lambda e, m=m, gsf=gsf, usf=usf: e.tensor_mul(
                big[:, m * 512:(m + 1) * 512], gu[:, gsf:gsf + 512],
                gu[:, usf:usf + 512]), inc="vs")
            free[("gst", p)] = ("vs", tok_fuse_last)
        tok_mm_d = None
        tok_devac = None
        for mo in range(16):
            q = mo % 2
            for sub in range(4):
                ss2 = (mo * 4 + sub) % 2
                dsl = 8192 + ss2 * 2048
                kd = f"wd{ss2}"
                wait_free("gpsimd", ("dslab", ss2))
                td = b.do("gpsimd", lambda e, mo=mo, sub=sub, dsl=dsl: e.dma_start(
                    out=wsb[:, dsl:dsl + 2048], in_=wd[mo, sub]), inc=kd, n=16)
                b.wait("tensor", kd, td)
                if mo == 0 and sub == 0:
                    b.wait("tensor", "vs", tok_fuse_last)
                if sub == 0:
                    wait_free("tensor", ("pCh", q))
                for kk in range(16):
                    k = sub * 16 + kk
                    tok_mm_d = b.do("tensor", lambda e, k=k, kk=kk, q=q, dsl=dsl: e.matmul(
                        pC[:, q * 512:(q + 1) * 512],
                        wsb[:, dsl + kk * 128:dsl + (kk + 1) * 128],
                        big[:, k * 512:(k + 1) * 512],
                        start=(k == 0), stop=(k == MCG - 1)),
                        inc="ts" if kk == 15 else None)
                free[("dslab", ss2)] = ("ts", tok_mm_d)
            b.wait("vector", "ts", tok_mm_d)
            xcol = mo * 1024 + blk * 512
            tok_devac = b.do("vector", lambda e, q=q, xcol=xcol, mo=mo: e.scalar_tensor_tensor(
                xT[:, xcol:xcol + 512], pC[:, q * 512:(q + 1) * 512],
                svec[:, SD + mo:SD + mo + 1], xT[:, xcol:xcol + 512],
                ALU.mult, ALU.add), inc="vs")
            free[("pCh", q)] = ("vs", tok_devac)
        free["fuse"] = ("ts", tok_mm_d)
        return tok_devac, tok_mm_d, tok_gu_mm_last

    tok_xn_free = None
    tok_ffn_evac = tok_ffn_mm = None
    for blk in (0, 1):
        tok_xn = emit_ln(blk, tok_xn_free)
        tok_ffn_evac, tok_ffn_mm, tok_xn_free = emit_ffn_block(blk, tok_xn)
    tok_ffn1_mm = tok_ffn_mm

    def emit_store(waits):
        for s, t in waits:
            b.wait("sync", s, t)
        b.do("sync", lambda e: e.dma_start(
            out=yTd.rearrange("(kc p) t -> p kc t", p=128),
            in_=xT[:, :].rearrange("p (kc t) -> p kc t", kc=KC)), inc="sto", n=16)
        b.wait("sync", "sto", 16)

    if STAGE == "ffn1":
        emit_store([("vs", tok_ffn_evac)])
        b.emit_block()
        es.close()
        return nc

    tok_qev = {}
    tok_qkv_mm = None
    qi = 0

    def emit_qkv_m(blk, m, tok_xn2, chase=False):
        nonlocal tok_qkv_mm, qi
        p = qi % 4
        h = qi % 2
        qi += 1
        qsl = p * 2048
        kq = f"wq{p}"
        if qi <= 4:
            b.wait("gpsimd", "ts", tok_ffn1_mm)
        wait_free("gpsimd", ("qslab", p))
        tq = b.do("gpsimd", lambda e, m=m, qsl=qsl: e.dma_start(
            out=wsb[:, qsl:qsl + 2048], in_=wq[m]), inc=kq, n=16)
        b.wait("tensor", kq, tq)
        if not chase:
            b.wait("tensor", "vs",
                   tok_xn2[-1] if isinstance(tok_xn2, list) else tok_xn2)
        wait_free("tensor", ("pCh", h))
        for k in range(KC):
            if chase and isinstance(tok_xn2, list):
                b.wait("tensor", "vs", tok_xn2[k])
            tok_qkv_mm = b.do("tensor", lambda e, k=k, p=h, qsl=qsl: e.matmul(
                pC[:, p * 512:(p + 1) * 512],
                wsb[:, qsl + k * 128:qsl + (k + 1) * 128],
                xn[:, k * 512:(k + 1) * 512],
                start=(k == 0), stop=(k == KC - 1)),
                inc="ts" if k == KC - 1 else None)
        free[("qslab", p)] = ("ts", tok_qkv_mm)
        if m < 16:
            darr, dcol = big, m * 1024 + blk * 512
        elif m == 16:
            darr, dcol = sbf, 1024 + blk * 512
        else:
            darr, dcol = sbf, 3072 + blk * 512
        b.wait("scalar", "ts", tok_qkv_mm)
        tok_ev = b.do("scalar", lambda e, m=m, p=h, darr=darr, dcol=dcol: e.activation(
            darr[:, dcol:dcol + 512], pC[:, p * 512:(p + 1) * 512],
            AF.Identity, bias=svec[:, BQ + m:BQ + m + 1]), inc="ss")
        free[("pCh", h)] = ("ss", tok_ev)
        tok_qev[(blk, m)] = tok_ev

    tok_xn2_blk = {}
    tok_xn2_blk[0] = emit_ln(0, tok_xn_free)
    first = True
    for m in (16, 17) + tuple(range(16)):
        emit_qkv_m(0, m, tok_xn2_blk[0], chase=first)
        first = False
    tok_xn_free = tok_qkv_mm
    tok_xn2_blk[1] = emit_ln(1, tok_xn_free)
    first = True
    for m in (16, 17):
        emit_qkv_m(1, m, tok_xn2_blk[1], chase=first)
        first = False

    b.wait("tensor", "gs", tok_ident)
    b.wait("tensor", "ss", tok_qev[(0, 17)])
    b.wait("tensor", "ss", tok_qev[(1, 17)])
    tok_tp = None
    for j in range(8):
        tok_tp = b.do("tensor", lambda e, j=j: e.transpose(
            pT[:, j * 128:(j + 1) * 128],
            sbf[:, 3072 + j * 128:3072 + (j + 1) * 128], ident[:]), inc="ts")
    b.wait("vector", "ts", tok_tp)
    tok_vtc = b.do("vector", lambda e: e.tensor_copy(
        sbf[:, 2048:3072], pT[:, 0:1024]), inc="vs")
    b.wait("sync", "ss", tok_qev[(1, 16)])
    b.wait("sync", "ss", tok_qev[(0, 16)])
    b.wait("sync", "vs", tok_vtc)
    b.do("sync", lambda e: e.dma_start(out=ccin.ap(), in_=sbf[:, 1024:3072]),
         inc="ccd", n=16)

    for m in range(2):
        emit_qkv_m(1, m, tok_xn2_blk[1])

    b.wait("gpsimd", "ccd", 16)
    b.do("gpsimd", lambda e: e.collective_compute(
        "AllGather", ALU.bypass, replica_groups=[list(range(NC))],
        ins=[ccin.ap().opt()], outs=[ccout.ap().opt()]), inc="ccs")
    b.wait("gpsimd", "ts", tok_ffn1_mm)
    tok_vmset = b.do("gpsimd", lambda e: e.memset(big[:, VOFF:BIGC], 1.0),
                     inc="gs")

    for m in range(2, 16):
        emit_qkv_m(1, m, tok_xn2_blk[1])
    tok_xn_free = tok_qkv_mm

    b.wait("gpsimd", "ts", tok_qkv_mm)
    tok_masks = None
    for d in range(4):
        tok_ms = b.do("gpsimd", lambda e, d=d: e.memset(
            xn[:, d * 512:(d + 1) * 512], 1.0), inc="gs")
        b.wait("gpsimd", "gs", tok_ms)
        tok_masks = b.do("gpsimd", lambda e, d=d: e.affine_select(
            out=xn[:, d * 512:(d + 1) * 512], in_=xn[:, d * 512:(d + 1) * 512],
            compare_op=ALU.is_ge, fill=0.0, base=-(128 * d),
            pattern=[[1, 512]], channel_multiplier=-1), inc="gs")

    b.wait("sync", "ccs", 1)
    b.wait("sync", "ts", tok_ffn1_mm)
    b.wait("sync", "gs", tok_vmset)
    co = ccout.ap()
    for r in range(NC):
        for jb, gb in ((0, r), (1, 15 - r)):
            b.do("sync", lambda e, r=r, jb=jb, gb=gb: e.dma_start(
                out=big[:, KOFF + gb * 512:KOFF + gb * 512 + 512],
                in_=co[r * 128:(r + 1) * 128, jb * 512:(jb + 1) * 512]),
                inc="krb", n=16)
    for r in range(NC):
        for jb, gb in ((0, r), (1, 15 - r)):
            for hf in range(2):
                b.do("sync", lambda e, r=r, jb=jb, gb=gb, hf=hf: e.dma_start(
                    out=big[:, VOFF + 4 * gb * 130:VOFF + 4 * gb * 130 + 520]
                        .rearrange("p (j r) -> p j r", r=130)
                        [:, :, hf * 65:hf * 65 + 64],
                    in_=co[r * 128:(r + 1) * 128,
                           1024 + jb * 512:1024 + (jb + 1) * 512]
                        .rearrange("p (j r) -> p j r", r=128)
                        [:, :, hf * 64:hf * 64 + 64]),
                    inc="krv", n=16)
    pre_tot = {s: b.cnt[s] for s in ("ts", "vs", "ss", "gs", "krb", "ccd")}
    krv_total = b.cnt["krv"]

    b.emit_block()

    sems = b.sems
    engines = [nc.tensor, nc.scalar, nc.vector, nc.gpsimd]
    idx = [nc.partition_id(engines=[ET.PE]),
           nc.partition_id(engines=[ET.Activation]),
           nc.partition_id(engines=[ET.DVE]),
           nc.partition_id(engines=[ET.Pool])]

    for body in nc.Switch(engines=engines, index=idx, n=NC):
        emit_attention_body(nc, sems, body, pre_tot, krv_total, xT, xn,
                            big, wsb, gu, sbf, st, svec, ones_bf1,
                            pA, pB, pC, pD, pT, wp)

    if STAGE == "attn":
        b.wait("sync", "at_done", 4)
        emit_store([])
        b.emit_block()
        es.close()
        return nc

    for key in [("pAh", 0), ("pAh", 1), ("pBh", 0), ("pBh", 1),
                ("pCh", 0), ("pCh", 1), "fuse", "st",
                ("gslab", 0), ("gslab", 1), ("uslab", 0), ("uslab", 1),
                ("dslab", 0), ("dslab", 1), ("sq", 0), ("sq", 1),
                ("gst", 0), ("gst", 1)]:
        free[key] = None
    tok_xn_free = tok_qkv_mm
    for blk in (0, 1):
        tok_xn = emit_ln(blk, tok_xn_free)
        tok_ffn_evac, tok_ffn_mm, tok_xn_free = emit_ffn_block(blk, tok_xn)

    emit_store([("vs", tok_ffn_evac)])
    b.emit_block()
    es.close()
    return nc


def emit_attention_body(nc, sems, c, pre_tot, krv_total, xT, xn,
                        big, wsb, gu, sbf, st, svec, ones_bf1,
                        pA, pB, pC, pD, pT, wp):
    at = {k: 0 for k in ("t", "v", "s", "g", "d0", "d1", "d2", "d3")}
    S_ = {"t": sems["at_t"], "v": sems["at_v"], "s": sems["at_s"],
          "g": sems["at_g"], "d0": sems["at_d0"], "d1": sems["at_d1"],
          "d2": sems["at_d2"], "d3": sems["at_d3"]}
    E = {"t": nc.tensor, "v": nc.vector, "s": nc.scalar, "g": nc.gpsimd}

    def inc(key, ins, n=1):
        at[key] += n
        ins.then_inc(S_[key], n)
        return at[key]

    def w(ekey, skey, tok):
        if tok and tok > 0:
            E[ekey].wait_ge(S_[skey], tok)

    for ekey in ("t", "s", "v", "g"):
        for s, v in pre_tot.items():
            if v > 0:
                E[ekey].wait_ge(sems[s], v)
    st_body = {"krv_waited": False}

    fre = {}

    def wfree(ekey, key):
        ent = fre.get(key)
        if ent is not None:
            w(ekey, ent[0], ent[1])

    pending = []
    for qbl, gb in ((0, c), (1, 15 - c)):
        nk = 4 * (gb + 1)
        for h in range(NH):
            qh = big[:, h * 1024 + qbl * 512: h * 1024 + qbl * 512 + 512]
            pvb = pB if h % 2 == 0 else pC
            pvname = ("pv", h % 2)

            def sc_ap(kt):
                r = kt % 3
                return pD[:, 0:512] if r == 2 else pA[:, r * 512:(r + 1) * 512]

            def scores(kt):
                wfree("t", ("scr", kt % 3))
                ins = nc.tensor.matmul(
                    sc_ap(kt), big[:, KOFF + kt * 128:KOFF + (kt + 1) * 128],
                    qh, start=True, stop=True)
                return inc("t", ins)

            tok_sc = {0: scores(0), 1: scores(1)}
            while pending:
                pending.pop(0)()
            tok_exp = {}
            tok_mask = {}
            tok_pv = None

            def emit_pv(kt):
                nonlocal tok_pv
                sl = kt % 4
                if not st_body["krv_waited"]:
                    nc.tensor.wait_ge(sems["krv"], krv_total)
                    st_body["krv_waited"] = True
                w("t", "s", tok_exp[kt])
                if kt in tok_mask:
                    w("t", "v", tok_mask[kt])
                if kt == 0:
                    wfree("t", pvname)
                va = VOFF + kt * 130
                nc.tensor.matmul(pvb[0:65, 0:512], big[:, va:va + 65],
                                 sbf[:, sl * 512:(sl + 1) * 512],
                                 start=(kt == 0), stop=(kt == nk - 1))
                ins = nc.tensor.matmul(pvb[0:65, 512:1024],
                                       big[:, va + 65:va + 130],
                                       sbf[:, sl * 512:(sl + 1) * 512],
                                       start=(kt == 0), stop=(kt == nk - 1))
                tok_pv = inc("t", ins)
                fre[("probs", sl)] = ("t", tok_pv)

            for kt in range(nk):
                sl = kt % 4
                w("s", "t", tok_sc[kt])
                wfree("s", ("probs", sl))
                ins = nc.scalar.activation(
                    sbf[:, sl * 512:(sl + 1) * 512],
                    sc_ap(kt), AF.Exp, scale=SCALE)
                tok_exp[kt] = inc("s", ins)
                fre[("scr", kt % 3)] = ("s", tok_exp[kt])
                if kt >= nk - 4:
                    d = kt - (nk - 4)
                    w("v", "s", tok_exp[kt])
                    ins = nc.vector.tensor_mul(
                        sbf[:, sl * 512:(sl + 1) * 512],
                        sbf[:, sl * 512:(sl + 1) * 512],
                        xn[:, d * 512:(d + 1) * 512])
                    tok_mask[kt] = inc("v", ins)
                if kt + 2 < nk:
                    tok_sc[kt + 2] = scores(kt + 2)
                if kt >= 1:
                    emit_pv(kt - 1)
            emit_pv(nk - 1)
            w("v", "t", tok_pv)
            wfree("v", ("strow", h % 2))
            rrow = sbf[0:1, 2048 + (h % 2) * 512:2048 + (h % 2) * 512 + 512]
            with nc.allow_low_precision(reason="softmax 1/sum rounded to bf16"):
                ins = nc.vector.reciprocal(rrow, pvb[64:65, 0:512])
            tok_rec = inc("v", ins)

            def flush_norm(h=h, pvb=pvb, pvname=pvname, tok_rec=tok_rec,
                           rrow=rrow):
                w("t", "v", tok_rec)
                wfree("t", ("scr", 2))
                ins = nc.tensor.matmul(pD[:, 0:512], ones_bf1[:], rrow,
                                       start=True, stop=True)
                tok_bc = inc("t", ins)
                fre[("strow", h % 2)] = ("t", tok_bc)
                w("v", "t", tok_bc)
                ins = nc.vector.tensor_copy(gu[:, 0:512], pD[:, 0:512])
                tok_rb = inc("v", ins)
                fre[("scr", 2)] = ("v", tok_rb)
                oc = OSB + h * 512
                w("v", "v", tok_rb)
                ins = nc.vector.tensor_mul(
                    wsb[0:64, oc:oc + 512], pvb[0:64, 0:512], gu[0:64, 0:512])
                inc("v", ins)
                ins = nc.vector.tensor_mul(
                    wsb[64:128, oc:oc + 512], pvb[0:64, 512:1024],
                    gu[64:128, 0:512])
                tok_norm = inc("v", ins)
                fre[pvname] = ("v", tok_norm)
            pending.append(flush_norm)
        while pending:
            pending.pop(0)()
        PSL = (0, 2048, 8192, 10240)
        for m in range(16):
            p = m % 4
            hh = m % 2
            dkey = f"d{p}"
            wfree("g", ("pslab", p))
            ins = nc.gpsimd.dma_start(out=wsb[:, PSL[p]:PSL[p] + 2048],
                                      in_=wp[m])
            tok_w = inc(dkey, ins, 16)
            w("t", dkey, tok_w)
            if m == 0:
                w("t", "s", at["s"])
                w("t", "v", at["v"])
            wfree("t", ("pAp", hh))
            for k in range(KC):
                ins = nc.tensor.matmul(
                    pA[:, hh * 512:(hh + 1) * 512],
                    wsb[:, PSL[p] + k * 128:PSL[p] + (k + 1) * 128],
                    wsb[:, OSB + k * 512:OSB + (k + 1) * 512],
                    start=(k == 0), stop=(k == KC - 1))
            tok_pm = inc("t", ins)
            fre[("pslab", p)] = ("t", tok_pm)
            w("v", "t", tok_pm)
            xcol = m * 1024 + qbl * 512
            ins = nc.vector.scalar_tensor_tensor(
                xT[:, xcol:xcol + 512],
                pA[:, hh * 512:(hh + 1) * 512],
                svec[:, BP + m:BP + m + 1],
                xT[:, xcol:xcol + 512],
                ALU.add, ALU.add)
            tok_pe = inc("v", ins)
            fre[("pAp", hh)] = ("v", tok_pe)
        fre[("scr", 0)] = ("v", at["v"])
        fre[("scr", 1)] = ("v", at["v"])

    nc.tensor.wait_ge(S_["t"], at["t"])
    nc.scalar.wait_ge(S_["s"], at["s"])
    nc.vector.wait_ge(S_["v"], at["v"])
    nc.gpsimd.wait_ge(S_["g"], at["g"])
    for eng in (nc.tensor, nc.scalar, nc.vector, nc.gpsimd):
        eng.sem_inc(sems["at_done"], 1)
    for eng in (nc.tensor, nc.scalar, nc.vector, nc.gpsimd):
        eng.wait_ge(sems["at_done"], 4)


def _shard(inputs):
    x = np.asarray(inputs["L0_attn_x"], dtype=np.float32).reshape(8192, H)
    xT = np.ascontiguousarray(x.T)
    wgate = np.asarray(inputs["L0_ffn_w_gate"])
    wup = np.asarray(inputs["L0_ffn_w_up"])
    wdown = np.asarray(inputs["L0_ffn_w_down"])
    Wq = np.asarray(inputs["L0_attn_W_qkv"], dtype=np.float32)
    Wp = np.asarray(inputs["L0_attn_W_proj"], dtype=np.float32)

    wg_only = wgate.reshape(16, 128, 64, 128).transpose(2, 1, 0, 3).reshape(64, 1, 128, 2048)
    wu_only = wup.reshape(16, 128, 64, 128).transpose(2, 1, 0, 3).reshape(64, 1, 128, 2048)
    wg_t = np.ascontiguousarray(np.concatenate([wg_only, wu_only], axis=1))
    wd_t = np.ascontiguousarray(
        wdown.reshape(64, 128, 16, 128).transpose(2, 1, 0, 3)
        .reshape(16, 128, 4, 2048).transpose(0, 2, 1, 3))
    wq_t = np.ascontiguousarray(
        Wq.reshape(16, 128, 18, 128).transpose(2, 1, 0, 3).reshape(18, 128, 2048))
    wp_t = np.ascontiguousarray(
        Wp.reshape(16, 128, 16, 128).transpose(2, 1, 0, 3).reshape(16, 128, 2048))

    vec = np.zeros((128, 180), dtype=np.float32)
    vec[:, SG:SG + 64] = np.asarray(inputs["L0_ffn_s_gate"],
                                    dtype=np.float32).reshape(64, 128).T
    vec[:, SU:SU + 64] = np.asarray(inputs["L0_ffn_s_up"],
                                    dtype=np.float32).reshape(64, 128).T
    vec[:, SD:SD + 16] = np.asarray(inputs["L0_ffn_s_down"],
                                    dtype=np.float32).reshape(16, 128).T
    vec[:, BQ:BQ + 18] = np.asarray(inputs["L0_attn_b_qkv"],
                                    dtype=np.float32).reshape(18, 128).T
    vec[:, BP:BP + 16] = np.asarray(inputs["L0_attn_b_proj"],
                                    dtype=np.float32).reshape(16, 128).T
    vec[:, 178] = EPS

    in_maps = []
    for c in range(NC):
        xTc = np.ascontiguousarray(np.concatenate(
            [xT[:, 512 * c:512 * (c + 1)],
             xT[:, 512 * (15 - c):512 * (16 - c)]], axis=1))
        in_maps.append(dict(xT=xTc, wg=wg_t, wd=wd_t, wq=wq_t,
                            wp=wp_t, vecs=vec))
    return in_maps


_NC_CACHE = {}


def _tune_compiler_flags():
    try:
        from concourse.compiler_utils import get_compiler_flags, set_compiler_flags
        flags = get_compiler_flags()
        out = []
        for f in flags:
            if f.startswith("--internal-backend-options="):
                f = f.replace("--enable-ldw-opt=false", "--enable-ldw-opt=true")
            out.append(f)
        set_compiler_flags(out)
    except Exception:
        pass


def kernel(**inputs):
    _tune_compiler_flags()
    in_maps = _shard(inputs)
    if STAGE not in _NC_CACHE:
        _NC_CACHE[STAGE] = build_nc()
    nc = _NC_CACHE[STAGE]
    res = run_bass_kernel_spmd(nc, in_maps, core_ids=list(range(NC)))
    yT = np.empty((H, 8192), dtype=np.float32)
    for c in range(NC):
        yc = res.results[c]["yT"]
        yT[:, 512 * c:512 * (c + 1)] = yc[:, 0:512]
        yT[:, 512 * (15 - c):512 * (16 - c)] = yc[:, 512:1024]
    return np.ascontiguousarray(yT.T).reshape(1, 8192, H)
